# version 21
# baseline (speedup 1.0000x reference)
import math
import numpy as np
import concourse.bass as bass
import concourse.mybir as mybir
from concourse.bass_utils import run_bass_kernel_spmd

F32 = mybir.dt.float32
BF16 = mybir.dt.bfloat16
AF = mybir.ActivationFunctionType
ALU = mybir.AluOpType
AX = mybir.AxisListType

B, N, D = 2, 4096, 2048
H = 16
T = 1024
NCH = 8
NB = 16
NE = 8
TT = 2
TN = 512
EPS = 1e-5
BIGNEG = -1e30

ENGS = ("tensor", "vector", "scalar", "gpsimd", "sync")
SEMS = ("dWa", "dWb", "dWc", "dWd", "dWe", "dWf", "dI", "dC1", "dAT", "dBV",
        "dT8", "dCC", "dL5a", "dL5b", "dWE", "dOa", "dOb", "dX", "pe", "ve",
        "ac", "po", "cc")
DMA_SEMS = ("dWa", "dWb", "dWc", "dWd", "dWe", "dWf", "dI", "dC1", "dAT",
            "dBV", "dT8", "dCC", "dL5a", "dL5b", "dWE", "dOa", "dOb", "dX")
OWN = {"tensor": ("pe",), "vector": ("ve",), "scalar": ("ac",),
       "gpsimd": ("po",) + DMA_SEMS, "sync": DMA_SEMS}

_TAPS = ()
_PHASE_LIMIT = 99


def _slopes(n):
    start = 2.0 ** (-2.0 ** (-(math.log2(n) - 3)))
    return np.array([start ** (i + 1) for i in range(n)], dtype=np.float64)


LAM = np.exp(-_slopes(H))


class Plan:
    def __init__(self):
        self.ops = {e: [] for e in ENGS}
        self.ctr = {s: 0 for s in SEMS}

    def cnt(self, s):
        return self.ctr[s]

    def emit(self, eng, builder, waits=(), inc=None, inc_n=1):
        if inc is not None:
            self.ctr[inc] += inc_n
        self.ops[eng].append((builder, tuple(waits), inc, inc_n))
        return self.ctr[inc] if inc else None

    def gate(self):
        return dict(self.ctr)


def build_program():
    nc = bass.Bass("TRN2", target_bir_lowering=False)

    xT_h = nc.dram_tensor("xT", [NB, 128, T], F32, kind="ExternalInput")
    wk_h = nc.dram_tensor("wk", [8, 16, 128, 256], F32, kind="ExternalInput")
    wv_h = nc.dram_tensor("wv", [8, 16, 128, 256], F32, kind="ExternalInput")
    wq_h = nc.dram_tensor("wq", [8, 16, 128, 256], F32, kind="ExternalInput")
    wo_h = nc.dram_tensor("wo", [8, 16, 128, 256], F32, kind="ExternalInput")
    bq_h = nc.dram_tensor("bq", [128, 16], F32, kind="ExternalInput")
    bk_h = nc.dram_tensor("bk", [128, 16], F32, kind="ExternalInput")
    bv_h = nc.dram_tensor("bv", [1, 2048], F32, kind="ExternalInput")
    bo_h = nc.dram_tensor("bo", [128, 16], F32, kind="ExternalInput")
    gw_h = nc.dram_tensor("gw", [22, 16, 128, 256], F32, kind="ExternalInput")
    dwt_h = nc.dram_tensor("dwt", [16, 22, 128, 128], F32, kind="ExternalInput")
    up_h = nc.dram_tensor("up", [NE, 11, 16, 128, 256], F32, kind="ExternalInput")
    dn_h = nc.dram_tensor("dn", [NE, 8, 11, 128, 256], F32, kind="ExternalInput")
    gate_h = nc.dram_tensor("gatew", [128, 16, 8], F32, kind="ExternalInput")
    gbias_h = nc.dram_tensor("gbias", [128, 8], F32, kind="ExternalInput")
    coeff_h = nc.dram_tensor("coeff", [8, 128, 2048], BF16, kind="ExternalInput")
    y_h = nc.dram_tensor("y", [D, T], F32, kind="ExternalOutput")

    cc_in = nc.dram_tensor("cc_in", [128, 2048], BF16, kind="Internal")
    cc_out = nc.dram_tensor("cc_out", [1024, 2048], BF16, kind="Internal",
                            addr_space="Shared")

    t_i = np.arange(128, dtype=np.float64)
    dq_np = np.zeros((128, H * 128), np.float64)
    dk_np = np.zeros((128, H * 128), np.float64)
    dm_np = np.zeros((128, H * 128), np.float64)
    l128_np = np.zeros((128, H * 128), np.float64)
    jj, ii = t_i[:, None], t_i[None, :]
    for h in range(H):
        dq_np[:, h * 128:(h + 1) * 128] = (LAM[h] ** (t_i + 1.0))[None, :]
        dk_np[:, h * 128:(h + 1) * 128] = (LAM[h] ** (127.0 - t_i))[None, :]
        l128_np[:, h * 128:(h + 1) * 128] = LAM[h] ** 128.0
        dm_np[:, h * 128:(h + 1) * 128] = np.where(ii >= jj, LAM[h] ** (ii - jj), 0.0)
    dkcol_np = np.zeros((128, 16), np.float64)
    dkc8_np = np.zeros((128, 8 * 16), np.float64)
    for h in range(H):
        dkcol_np[:, h] = LAM[h] ** (127.0 - t_i)
        for c in range(8):
            dkc8_np[:, c * 16 + h] = LAM[h] ** (127.0 - t_i + 128.0 * (7 - c))
    dq_c = nc.inline_tensor(dq_np.astype(np.float32), name="dq_c")
    dm_c = nc.inline_tensor(dm_np.astype(np.float32), name="dm_c")
    dkcol_c = nc.inline_tensor(dkcol_np.astype(np.float32), name="dkcol_c")
    dkc8_c = nc.inline_tensor(dkc8_np.astype(np.float32), name="dkc8_c")
    dkc8t_np = np.zeros((8, 128, H * 128), np.float64)
    for h in range(H):
        for c in range(8):
            dkc8t_np[c, :, h * 128:(h + 1) * 128] = \
                (LAM[h] ** (127.0 - t_i + 128.0 * (7 - c)))[:, None]
    dkc8t_c = nc.inline_tensor(dkc8t_np.astype(np.float32), name="dkc8t_c")
    ident_c = nc.inline_tensor(np.eye(128, dtype=np.float32), name="ident_c")
    onescol_c = nc.inline_tensor(np.ones((128, 1), np.float32), name="onescol_c")
    onesrow_c = nc.inline_tensor(np.ones((1, 128), np.float32), name="onesrow_c")
    eps_c = nc.inline_tensor(np.full((1, 1), EPS, np.float32), name="eps_c")

    KB = 1024
    BO = (nc.sbuf_base + 31) // 32 * 32

    def sb(name, off, shape, dtype):
        import concourse.mybir as _mb
        end = BO + off + shape[1] * _mb.dt.size(dtype)
        assert end <= 229344, (name, off, end)
        return nc.alloc_sbuf_tensor_at(name, shape, dtype, offset=BO + off)

    xbf = sb("xbf", 0 * KB, [128, NB * T], BF16)
    lnb = sb("lnb", 32 * KB, [128, NB * T], BF16)
    qfm = sb("qfm", 64 * KB, [128, NB * T], BF16)
    kfm = sb("kfm", 96 * KB, [128, NB * T], BF16)
    vtm = sb("vtm", 128 * KB, [128, NCH * D], BF16)
    acc = sb("acc", 64 * KB, [128, NB * T], BF16)
    hbuf = sb("hbuf", 128 * KB, [128, 22 * T], BF16)
    we_fm = sb("we_fm", 96 * KB, [8, T], BF16)
    we_row = sb("we_row", 100 * KB, [1, T], BF16)
    we_b = [sb("we_b0", 104 * KB, [128, T], BF16),
            sb("we_b1", 108 * KB, [128, T], BF16)]
    outst = sb("outst", 112 * KB, [128, 2 * T], F32)
    ktm_s = sb("ktm_s", 160 * KB, [128, H * 128], BF16)
    Z_bf = sb("Z_bf", 164 * KB, [128, H * 128], BF16)
    dq_t = sb("dq_t", 168 * KB, [128, H * 128], BF16)
    dkc8b = sb("dkc8b", 168 * KB, [128, H * 128], BF16)
    dm_t = sb("dm_t", 172 * KB, [128, H * 128], BF16)
    S_f = sb("S_f", 176 * KB, [128, H * 128], F32)
    rb_s = sb("rb_s", 160 * KB, [128, T], BF16)
    ab_s = sb("ab_s", 164 * KB, [128, T], BF16)
    rows = sb("rows", 168 * KB, [1, 3 * T], F32)
    rows_bf = sb("rows_bf", 180 * KB, [1, 2 * T], BF16)
    Lbuf = sb("Lbuf", 160 * KB, [128, H * 128], F32)
    cbuf = sb("cbuf", 168 * KB, [128, H * 128], BF16)
    bv_s = sb("bv_s", 168 * KB, [1, 2048], BF16)
    M0 = 184 * KB
    bq_s = sb("bq_s", M0 + 0, [128, 16], F32)
    bk_s = sb("bk_s", M0 + 64, [128, 16], F32)
    bo_s = sb("bo_s", M0 + 128, [128, 16], F32)
    ident_s = sb("ident_s", M0 + 192, [128, 128], BF16)
    onescol_s = sb("onescol_s", M0 + 448, [128, 1], BF16)
    onesrow_s = sb("onesrow_s", M0 + 480, [1, 128], BF16)
    onesrowf_s = sb("onesrowf_s", M0 + 736, [1, 128], F32)
    gbias_s = sb("gbias_s", M0 + 1248, [128, 8], F32)
    gatew_s = sb("gatew_s", M0 + 1280, [128, 16 * 8], BF16)
    gsig_s = sb("gsig_s", M0 + 1536, [128, 64], F32)
    wn_s = sb("wn_s", M0 + 1792, [128, 64], BF16)
    g1_s = sb("g1_s", M0 + 1920, [128, 8], F32)
    g2_s = sb("g2_s", M0 + 1952, [128, 8], F32)
    g3_s = sb("g3_s", M0 + 1984, [128, 1], F32)
    g4_s = sb("g4_s", M0 + 2016, [128, 1], F32)
    eps_s = sb("eps_s", M0 + 2048, [1, 1], F32)
    dkcol_s = sb("dkcol_s", M0 + 2080, [128, 16], F32)
    dkc8_s = sb("dkc8_s", 206 * KB + 512, [128, 8 * 16], F32)
    sinit_bf = sb("sinit_bf", 186 * KB + 512, [128, H * 128], BF16)
    tmpA = sb("tmpA", 186 * KB + 512, [128, T], BF16)
    tmpB = sb("tmpB", 188 * KB + 512, [128, T], BF16)
    wbA = sb("wbA", 190 * KB + 512, [128, 16 * 256], BF16)
    wbB = sb("wbB", 198 * KB + 512, [128, 16 * 256], BF16)
    wb3 = sb("wb3", 120 * KB, [128, 16 * 256], BF16)
    wb4 = sb("wb4", 152 * KB, [128, 16 * 256], BF16)
    wbQ = sb("wbQ", 64 * KB, [128, 16 * 256], BF16)
    wbQ2 = sb("wbQ2", 72 * KB, [128, 16 * 256], BF16)
    wbQ3 = sb("wbQ3", 80 * KB, [128, 16 * 256], BF16)
    wbQ4 = sb("wbQ4", 88 * KB, [128, 16 * 256], BF16)
    wb = (wbA, wbB, wbQ, wbQ2, wbQ3, wbQ4)

    p32 = [nc.alloc_psum_tensor(f"p32_{i}", [128, 512], F32) for i in range(4)]
    trA = nc.alloc_psum_tensor("trA", [128, 1024], BF16)
    trB = nc.alloc_psum_tensor("trB", [128, 1024], BF16)
    pb6 = nc.alloc_psum_tensor("pb6", [128, 512], F32)
    pb7 = nc.alloc_psum_tensor("pb7", [128, 512], F32)

    P = Plan()

    def MM(out, lhsT, rhs, start, stop, waits=()):
        P.emit("tensor",
               lambda t, o=out, l=lhsT, r=rhs, s0=start, s1=stop:
               t.matmul(o, l, r, start=s0, stop=s1), waits=waits, inc="pe")
        return P.cnt("pe")

    def TRP(out, in_, waits=()):
        P.emit("tensor",
               lambda t, o=out, i=in_: t.transpose(o, i, ident_s[:, :]),
               waits=waits, inc="pe")
        return P.cnt("pe")

    def V(builder, waits=()):
        P.emit("vector", builder, waits=waits, inc="ve")
        return P.cnt("ve")

    def A_(builder, waits=()):
        P.emit("scalar", builder, waits=waits, inc="ac")
        return P.cnt("ac")

    def PO(builder, waits=()):
        P.emit("gpsimd", builder, waits=waits, inc="po")
        return P.cnt("po")

    def DMA_W(out, in_, waits=(), sem="dC1"):
        P.emit("gpsimd", lambda g, o=out, i=in_: g.dma_start(out=o, in_=i),
               waits=waits, inc=sem, inc_n=16)
        return P.cnt(sem)

    def DMA_X(out, in_, waits=(), sem="dX"):
        P.emit("sync", lambda s, o=out, i=in_: s.dma_start(out=o, in_=i),
               waits=waits, inc=sem, inc_n=16)
        return P.cnt(sem)

    def WAIT(eng, waits):
        if waits:
            P.emit(eng, None, waits=waits)

    def barrier(extra_excl=()):
        g = P.gate()
        for e in ENGS:
            excl = (set(OWN[e]) | {"dWa", "dWb", "dWc", "dWd", "dWe", "dWf",
                                   "cc"} | set(extra_excl))
            w = tuple((s, v) for s, v in g.items() if v > 0 and s not in excl)
            WAIT(e, w)

    def tap(name, ap):
        if name in _TAPS:
            th = nc.dram_tensor("tap_" + name, list(ap.shape), ap.tensor.dtype,
                                kind="ExternalOutput")
            g = P.gate()
            w = tuple((s, v) for s, v in g.items() if v > 0 and s != "dX")
            DMA_X(th[tuple(slice(0, s) for s in ap.shape)], ap, waits=w)

    def fmsl(tsr, blk, t0, t1):
        return tsr[:, blk * T + t0: blk * T + t1]

    def blk3(tsr, c):
        return tsr.rearrange("p (b t) -> p b t", b=NB)[:, :, c * 128:(c + 1) * 128]

    def hview(tsr):
        return tsr.rearrange("p (h t) -> p h t", h=H)

    ws = {"i": 0, "consumed": {}, "bufs": None, "nb": 6}

    def wbuf(i):
        return (ws["bufs"] or wb)[i % ws["nb"]]

    WSEMS = ("dWa", "dWb", "dWc", "dWd", "dWe", "dWf")

    def wload(src_ap, nkf):
        i = ws["i"]
        ws["i"] += 1
        nb = ws["nb"]
        waits = (("pe", ws["consumed"][i - nb]),) if i >= nb else ()
        waits = waits + tuple(ws.get("hold", ()))
        dst = wbuf(i)[:, :nkf].rearrange("p (k f) -> p k f", k=src_ap.shape[0])
        sem = WSEMS[i % nb]
        P.emit("gpsimd", lambda g, o=dst, s=src_ap: g.dma_start(out=o, in_=s),
               waits=waits, inc=sem, inc_n=16)
        return i, (sem, P.cnt(sem))

    def stream(tiles, consume, pre=None):
        nb = ws["nb"]
        pend = list(pre) if pre else []
        pend += [wload(*t) for t in tiles[len(pend):nb]]
        for j in range(len(tiles)):
            idx, dtar = pend[j]
            consume(j, idx, dtar)
            ws["consumed"][idx] = P.cnt("pe")
            if j + nb < len(tiles):
                pend.append(wload(*tiles[j + nb]))

    rot = {"r": 0, "evac": [None, None, None, None]}

    def rot_reset():
        rot["r"] = 0
        g = ("ve", P.cnt("ve")) if P.cnt("ve") else None
        rot["evac"] = [g, g, g, g]

    def next_bank():
        b = rot["r"] % 4
        rot["r"] += 1
        return b

    DMA_W(ident_s[:, :], ident_c[:, :])
    DMA_W(onescol_s[:, :], onescol_c[:, :])
    DMA_W(onesrow_s[:, :], onesrow_c[:, :])
    d_gatew = DMA_W(gatew_s[:, :], gate_h[:, :, :].rearrange("p k e -> p (k e)"))
    d_bfc = P.cnt("dC1")
    wk_tiles0 = [(wk_h[j, :, :, :].rearrange("k p f -> p k f"), 16 * 256)
                 for j in range(8)]
    wk_pre = []
    d_xg = {}
    for g4 in range(4):
        d4 = DMA_W(xbf[:, g4 * 4 * T:(g4 + 1) * 4 * T]
                   .rearrange("p (b t) -> p b t", b=4),
                   xT_h[g4 * 4:(g4 + 1) * 4, :, :].rearrange("b p t -> p b t"),
                   sem="dI")
        for cb in range(g4 * 4, g4 * 4 + 4):
            d_xg[cb] = ("dI", d4)
    for t_ in range(6):
        wk_pre.append(wload(*wk_tiles0[t_]))
    DMA_X(bq_s[:, :], bq_h[:, :])
    DMA_X(bk_s[:, :], bk_h[:, :])
    DMA_X(bo_s[:, :], bo_h[:, :])
    DMA_X(gbias_s[:, :], gbias_h[:, :])
    d_orow = DMA_X(onesrowf_s[:, :], onesrow_c[:, :])
    DMA_X(eps_s[:, :], eps_c[:, :])
    DMA_X(dkcol_s[:, :], dkcol_c[:, :])
    DMA_X(dkc8_s[:, :], dkc8_c[:, :])

    def ln_prepare(src, sq_ac=False, first_dw=None, blk_ready=None):
        mrow = rows[0:1, 0:T]
        vrow = rows[0:1, T:2 * T]
        trow = rows[0:1, 2 * T:3 * T]
        pe_start = P.cnt("pe")
        nmm = 0
        qi = 0
        sq_sem = "ac" if sq_ac else "ve"
        for tt in range(TT):
            for cb in range(NB):
                w = []
                if cb == 0 and tt == 0 and first_dw is not None:
                    w.append(first_dw)
                if blk_ready is not None:
                    w.append(blk_ready[cb])
                if qi >= 4:
                    w.append(("pe", pe_start + (qi - 4) * 2 + 2))
                half = qi % 4
                tS = (tmpA if half < 2 else tmpB)[:, (half % 2) * TN:
                                                  (half % 2 + 1) * TN]
                if sq_ac:
                    cS = A_(lambda s, c=cb, t_=tt, x_=tS: s.activation(
                        x_, fmsl(src, c, t_ * TN, (t_ + 1) * TN), AF.Square),
                        waits=tuple(w))
                else:
                    cS = V(lambda v, c=cb, t_=tt, x_=tS: v.tensor_tensor(
                        x_, fmsl(src, c, t_ * TN, (t_ + 1) * TN),
                        fmsl(src, c, t_ * TN, (t_ + 1) * TN), ALU.mult),
                        waits=tuple(w))
                w0 = []
                if cb == 0 and tt == 0:
                    w0.append(("dC1", d_bfc))
                    if first_dw is not None:
                        w0.append(first_dw)
                if blk_ready is not None:
                    w0.append(blk_ready[cb])
                prow = 0 if tt == 0 else 32
                MM(pb6[prow:prow + 1, :], onescol_s[:, :],
                   fmsl(src, cb, tt * TN, (tt + 1) * TN),
                   cb == 0, cb == NB - 1, waits=tuple(w0))
                MM(pb7[prow:prow + 1, :], onescol_s[:, :], tS,
                   cb == 0, cb == NB - 1, waits=((sq_sem, cS),))
                nmm += 2
                qi += 1
        pe_stats = P.cnt("pe")
        V(lambda v: v.tensor_copy(mrow[:, 0:TN], pb6[0:1, :]),
          waits=(("pe", pe_stats),))
        V(lambda v: v.tensor_copy(mrow[:, TN:2 * TN], pb6[32:33, :]))
        V(lambda v: v.tensor_copy(vrow[:, 0:TN], pb7[0:1, :]))
        V(lambda v: v.tensor_copy(vrow[:, TN:2 * TN], pb7[32:33, :]))
        V(lambda v: v.tensor_scalar(rows[0:1, 0:2 * T], rows[0:1, 0:2 * T],
                                    1.0 / D, None, ALU.mult))
        V(lambda v: v.tensor_tensor(trow[:, :], mrow[:, :], mrow[:, :],
                                    ALU.mult))
        e = V(lambda v: v.tensor_sub(vrow[:, :], vrow[:, :], trow[:, :]))
        a1 = A_(lambda s: s.activation(trow[:, :], vrow[:, :], AF.Sqrt,
                                       bias=eps_s[0:1, 0:1], scale=1.0),
                waits=(("ve", e),))
        er = V(lambda v: v.reciprocal(vrow[:, :], trow[:, :]),
               waits=(("ac", a1),))
        V(lambda v: v.tensor_tensor(trow[:, :], mrow[:, :], vrow[:, :],
                                    ALU.mult), waits=(("ve", er),))
        ec = V(lambda v: v.tensor_copy(rows_bf[:, :], rows[0:1, T:3 * T]))
        m1 = MM(pb6[:, :], onesrow_s[:, :], rows_bf[0:1, 0:TN], True, True,
                waits=(("ve", ec), ("dC1", d_bfc)))
        m2 = MM(pb7[:, :], onesrow_s[:, :], rows_bf[0:1, TN:2 * TN],
                True, True, waits=(("ve", ec),))
        b1 = V(lambda v: v.tensor_copy(rb_s[:, 0:TN], pb6[:, :]),
               waits=(("pe", m1),))
        b2 = V(lambda v: v.tensor_copy(rb_s[:, TN:2 * TN], pb7[:, :]),
               waits=(("pe", m2),))
        m3 = MM(pb6[:, :], onesrow_s[:, :], rows_bf[0:1, 2 * TN:3 * TN],
                True, True, waits=(("ve", b1), ("ve", ec)))
        m4 = MM(pb7[:, :], onesrow_s[:, :], rows_bf[0:1, 3 * TN:4 * TN],
                True, True, waits=(("ve", b2), ("ve", ec)))
        V(lambda v: v.tensor_copy(ab_s[:, 0:TN], pb6[:, :]), waits=(("pe", m3),))
        V(lambda v: v.tensor_copy(ab_s[:, TN:2 * TN], pb7[:, :]),
          waits=(("pe", m4),))
        return P.cnt("ve")

    def ln_apply_to(src, dst, ready):
        blks = {}
        first = {"vector": True, "gpsimd": True}
        for cb in range(NB):
            if cb % 2 == 0:
                w = (("ve", ready),) if first["vector"] else ()
                first["vector"] = False
                V(lambda v, c=cb: v.tensor_tensor(
                    fmsl(dst, c, 0, T), fmsl(src, c, 0, T), rb_s[:, :],
                    ALU.mult), waits=w)
                blks[cb] = ("ve", V(lambda v, c=cb: v.tensor_tensor(
                    fmsl(dst, c, 0, T), fmsl(dst, c, 0, T), ab_s[:, :],
                    ALU.subtract)))
            else:
                w = (("ve", ready),) if first["gpsimd"] else ()
                first["gpsimd"] = False
                PO(lambda g, c=cb: g.tensor_tensor(
                    fmsl(dst, c, 0, T), fmsl(src, c, 0, T), rb_s[:, :],
                    ALU.mult), waits=w)
                blks[cb] = ("po", PO(lambda g, c=cb: g.tensor_tensor(
                    fmsl(dst, c, 0, T), fmsl(dst, c, 0, T), ab_s[:, :],
                    ALU.subtract)))
        return blks

    ln1blk = {}
    if _PHASE_LIMIT >= 1:
        r = ln_prepare(xbf, sq_ac=True, blk_ready=d_xg)
        ln1blk = ln_apply_to(xbf, lnb, r)
        tap("ln1", lnb[:, :])
    ph1_ve = P.cnt("ve")

    def proj_fm(w_handle, src, dst, bias_col, kwaits=None, pre=None):
        blk_ac = {}

        def consume(j, idx, dtar):
            for fc in range(2):
                blk = j * 2 + fc
                for tt in range(TT):
                    b = next_bank()
                    ev = rot["evac"][b]
                    for k in range(16):
                        w = []
                        if j == 0 and fc == 0 and tt == 0 and kwaits is not None:
                            w.append(kwaits[k])
                        if k == 0:
                            w.append(dtar)
                            if ev is not None:
                                w.append(ev)
                        MM(p32[b][:, :],
                           wbuf(idx)[:, k * 256 + fc * 128:
                                       k * 256 + fc * 128 + 128],
                           fmsl(src, k, tt * TN, (tt + 1) * TN),
                           k == 0, k == 15, waits=tuple(w))
                    pe_c = P.cnt("pe")
                    cnt = A_(lambda s, b_=b, bl=blk, t_=tt: s.activation(
                        fmsl(dst, bl, t_ * TN, (t_ + 1) * TN), p32[b_][:, :],
                        AF.Identity, bias=bias_col[:, bl:bl + 1], scale=1.0),
                        waits=(("pe", pe_c),))
                    rot["evac"][b] = ("ac", cnt)
                    blk_ac[blk] = cnt
        tiles = [(w_handle[j, :, :, :].rearrange("k p f -> p k f"), 16 * 256)
                 for j in range(8)]
        stream(tiles, consume, pre=pre)
        return blk_ac

    if _PHASE_LIMIT >= 2:
        d_bv = DMA_W(bv_s[:, :], bv_h[:, :], sem="dBV",
                     waits=(("ve", ph1_ve), ("pe", P.cnt("pe"))))

        rot_reset()
        proj_fm(wk_h, lnb, kfm, bk_s, kwaits=ln1blk, pre=wk_pre)
        tap("kfm", kfm[:, :])

        def v_consume(j, idx, dtar):
            for tc in range(NCH):
                b = next_bank()
                ev = rot["evac"][b]
                for k in range(16):
                    w = []
                    if k == 0:
                        w.append(dtar)
                        if ev is not None:
                            w.append(ev)
                    MM(p32[b][:, 0:256],
                       fmsl(lnb, k, tc * 128, (tc + 1) * 128),
                       wbuf(idx)[:, k * 256:(k + 1) * 256],
                       k == 0, False, waits=tuple(w))
                MM(p32[b][:, 0:256], onesrow_s[:, :],
                   bv_s[0:1, j * 256:(j + 1) * 256], False, True,
                   waits=(("dBV", d_bv),))
                pe_c = P.cnt("pe")
                cnt = V(lambda v, b_=b, j_=j, t_=tc: v.tensor_copy(
                    vtm[:, t_ * D + j_ * 256: t_ * D + (j_ + 1) * 256],
                    p32[b_][:, 0:256]), waits=(("pe", pe_c),))
                rot["evac"][b] = ("ve", cnt)
        rot_reset()
        tiles = [(wv_h[j, :, :, :].rearrange("k p f -> p k f"), 16 * 256)
                 for j in range(8)]
        stream(tiles, v_consume)
        tap("vtm", vtm[:, :])
        ph2_pe = P.cnt("pe")
        barrier()

    att = {"inter": 0, "sdone": 0, "tr_evac": 0, "sc_evac": 0,
           "ao_evac": 0, "tr_done": 0, "u_prev": 0, "ladd": 0,
           "aoA": 0, "aoB": 0, "tab": 0, "zcast": 0}

    def chunk_L(c):
        if c == 0:
            att["tab"] = DMA_W(dkc8b[:, :], dkc8t_c[0, :, :], sem="dT8")
        for h in range(H):
            tr = trA if h < 8 else trB
            w = []
            if h == 0 and att["tr_evac"]:
                w.append(("ve", att["tr_evac"]))
            att["tr_done"] = TRP(tr[:, (h % 8) * 128:(h % 8 + 1) * 128],
                                 fmsl(kfm, h, c * 128, (c + 1) * 128),
                                 waits=tuple(w))
        w = [("pe", att["tr_done"]), ("dT8", att["tab"])]
        if att["u_prev"]:
            w.append(("pe", att["u_prev"]))
        V(lambda v: v.tensor_tensor(ktm_s[:, 0:1024], trA[:, :],
                                    dkc8b[:, 0:1024], ALU.mult),
          waits=tuple(w))
        att["tr_evac"] = V(lambda v: v.tensor_tensor(
            ktm_s[:, 1024:2048], trB[:, :], dkc8b[:, 1024:2048], ALU.mult))
        if c + 1 < NCH:
            att["tab"] = DMA_W(dkc8b[:, :], dkc8t_c[c + 1, :, :],
                               sem="dT8", waits=(("ve", att["tr_evac"]),))
        u_done = 0
        for h in range(H):
            w = []
            if h == 0:
                w.append(("ve", att["tr_evac"]))
                if att["ladd"]:
                    w.append(("ve", att["ladd"]))
            u_done = MM(p32[h // 4][:, (h % 4) * 128:(h % 4 + 1) * 128],
                        ktm_s[:, h * 128:(h + 1) * 128],
                        vtm[:, c * D + h * 128: c * D + (h + 1) * 128],
                        True, True, waits=tuple(w))
        att["u_prev"] = u_done
        for i in range(4):
            att["ladd"] = V(lambda v, i_=i: v.tensor_tensor(
                S_f[:, i_ * 512:(i_ + 1) * 512],
                S_f[:, i_ * 512:(i_ + 1) * 512], p32[i_][:, :], ALU.add),
                waits=(("pe", u_done),) if i == 0 else ())

    def emit_tr(c):
        for h in range(H):
            tr = trA if h < 8 else trB
            w = []
            if h == 0 and att["tr_evac"]:
                w.append((att.get("tr_evac_sem", "ve"), att["tr_evac"]))
            att["tr_done"] = TRP(tr[:, (h % 8) * 128:(h % 8 + 1) * 128],
                                 fmsl(kfm, h, c * 128, (c + 1) * 128),
                                 waits=tuple(w))

    def chunk_local(c, first_dw=None, first_dx=None):
        if c > 0:
            w = [("ve", att["sdone"]), ("pe", att["inter"])]
            if c == 1 and first_dx is not None:
                w.append(("dCC", first_dx))
            att["zcast"] = A_(lambda s: s.activation(Z_bf[:, :], S_f[:, :],
                                                     AF.Identity),
                              waits=tuple(w))
        if c == 0:
            emit_tr(c)
        w = [("pe", att["tr_done"])]
        if att["inter"]:
            w.append(("pe", att["inter"]))
        A_(lambda s: s.activation(ktm_s[:, 0:1024], trA[:, :], AF.Identity),
           waits=tuple(w))
        att["tr_evac"] = A_(lambda s: s.activation(
            ktm_s[:, 1024:2048], trB[:, :], AF.Identity))
        att["tr_evac_sem"] = "ac"
        kA = kB = 0
        for h in range(H):
            cnt = A_(lambda s, h_=h: s.activation(
                ktm_s[:, h_ * 128:(h_ + 1) * 128],
                ktm_s[:, h_ * 128:(h_ + 1) * 128], AF.Identity,
                scale=dkcol_s[:, h_:h_ + 1]))
            if h == 7:
                kA = cnt
            if h == 15:
                kB = cnt
        u_done = 0
        for h in range(H):
            w = []
            if h == 0:
                w.append(("ac", kA))
                if att["aoB"]:
                    w.append(("ve", att["aoB"]))
            if h == 8:
                w.append(("ac", kB))
            u_done = MM(p32[h // 4][:, (h % 4) * 128:(h % 4 + 1) * 128],
                        ktm_s[:, h * 128:(h + 1) * 128],
                        vtm[:, c * D + h * 128: c * D + (h + 1) * 128],
                        True, True, waits=tuple(w))
        scA = 0
        for h in range(8):
            pbank = pb6 if h < 4 else pb7
            w = []
            if h == 0:
                if att["aoA"]:
                    w.append(("ve", att["aoA"]))
                if first_dw is not None and c == 0:
                    w.append(("dAT", first_dw))
            scA = MM(pbank[:, (h % 4) * 128:(h % 4 + 1) * 128],
                     fmsl(kfm, h, c * 128, (c + 1) * 128),
                     fmsl(qfm, h, c * 128, (c + 1) * 128),
                     True, True, waits=tuple(w))
        if c + 1 < NCH:
            emit_tr(c + 1)
        w = [("pe", u_done)]
        if c > 0:
            w.append(("ac", att["zcast"]))
        smult = 0
        for h in range(H):
            smult = A_(lambda s, h_=h: s.activation(
                S_f[:, h_ * 128:(h_ + 1) * 128],
                S_f[:, h_ * 128:(h_ + 1) * 128], AF.Identity,
                scale=float(LAM[h_] ** 128.0)),
                waits=tuple(w) if h == 0 else ())
        for i in range(4):
            att["sdone"] = V(lambda v, i_=i: v.tensor_tensor(
                S_f[:, i_ * 512:(i_ + 1) * 512],
                S_f[:, i_ * 512:(i_ + 1) * 512], p32[i_][:, :], ALU.add),
                waits=(("ac", smult),) if i == 0 else ())
        scpA = 0
        for i in range(2):
            pbank = pb6 if i == 0 else pb7
            scpA = V(lambda v, p_=pbank, i_=i: v.tensor_tensor(
                ktm_s[:, i_ * 512:(i_ + 1) * 512], p_[:, :],
                dm_t[:, i_ * 512:(i_ + 1) * 512], ALU.mult),
                waits=(("pe", scA),) if i == 0 else ())
        scB = 0
        for h in range(8, 16):
            b = (h - 8) // 4
            w = [("ve", att["sdone"])] if h == 8 else []
            scB = MM(p32[b][:, (h % 4) * 128:(h % 4 + 1) * 128],
                     fmsl(kfm, h, c * 128, (c + 1) * 128),
                     fmsl(qfm, h, c * 128, (c + 1) * 128),
                     True, True, waits=tuple(w))
        qt = V(lambda v, c_=c: v.tensor_tensor(
            blk3(qfm, c_), blk3(qfm, c_), hview(dq_t), ALU.mult),
            waits=(("pe", scB),))
        scpB = 0
        for i in range(2):
            scpB = V(lambda v, i_=i: v.tensor_tensor(
                ktm_s[:, 1024 + i_ * 512:1024 + (i_ + 1) * 512],
                p32[i_][:, :], dm_t[:, 1024 + i_ * 512:1024 + (i_ + 1) * 512],
                ALU.mult), waits=(("pe", scB),) if i == 0 else ())
        lastA = 0
        for h in range(8):
            pbank = pb6 if h < 4 else pb7
            w = []
            if h == 0:
                w.append(("ve", scpA))
                w.append(("ve", qt))
                if c > 0:
                    w.append(("ac", att["zcast"]))
            if c > 0:
                MM(pbank[:, (h % 4) * 128:(h % 4 + 1) * 128],
                   Z_bf[:, h * 128:(h + 1) * 128],
                   fmsl(qfm, h, c * 128, (c + 1) * 128),
                   True, False, waits=tuple(w))
                lastA = MM(pbank[:, (h % 4) * 128:(h % 4 + 1) * 128],
                           vtm[:, c * D + h * 128: c * D + (h + 1) * 128],
                           ktm_s[:, h * 128:(h + 1) * 128], False, True)
            else:
                lastA = MM(pbank[:, (h % 4) * 128:(h % 4 + 1) * 128],
                           vtm[:, c * D + h * 128: c * D + (h + 1) * 128],
                           ktm_s[:, h * 128:(h + 1) * 128], True, True,
                           waits=tuple(w))
        for i in range(2):
            pbank = pb6 if i == 0 else pb7
            att["aoA"] = V(lambda v, p_=pbank, i_=i, c_=c: v.tensor_copy(
                lnb.rearrange("p (b t) -> p b t", b=NB)
                [:, 4 * i_:4 * i_ + 4, c_ * 128:(c_ + 1) * 128],
                p_[:, :].rearrange("p (h t) -> p h t", h=4)),
                waits=(("pe", lastA),) if i == 0 else ())
        lastB = 0
        for h in range(8, 16):
            b = 2 + (h - 8) // 4
            w = []
            if h == 8:
                w.append(("ve", scpB))
            if c > 0:
                MM(p32[b][:, (h % 4) * 128:(h % 4 + 1) * 128],
                   Z_bf[:, h * 128:(h + 1) * 128],
                   fmsl(qfm, h, c * 128, (c + 1) * 128),
                   True, False, waits=tuple(w))
                lastB = MM(p32[b][:, (h % 4) * 128:(h % 4 + 1) * 128],
                           vtm[:, c * D + h * 128: c * D + (h + 1) * 128],
                           ktm_s[:, h * 128:(h + 1) * 128], False, True)
            else:
                lastB = MM(p32[b][:, (h % 4) * 128:(h % 4 + 1) * 128],
                           vtm[:, c * D + h * 128: c * D + (h + 1) * 128],
                           ktm_s[:, h * 128:(h + 1) * 128], True, True,
                           waits=tuple(w))
        att["inter"] = lastB
        for i in range(2):
            att["aoB"] = V(lambda v, i_=i, c_=c: v.tensor_copy(
                lnb.rearrange("p (b t) -> p b t", b=NB)
                [:, 8 + 4 * i_:12 + 4 * i_, c_ * 128:(c_ + 1) * 128],
                p32[2 + i_][:, :].rearrange("p (h t) -> p h t", h=4)),
                waits=(("pe", lastB),) if i == 0 else ())
        att["ao_evac"] = att["aoB"]

    if _PHASE_LIMIT >= 3:
        V(lambda v: v.memset(S_f[:, :], 0.0))
        att["tr_done"] = att["tr_evac"] = att["u_prev"] = att["ladd"] = 0
        for c in range(NCH):
            chunk_L(c)
        ev = V(lambda v: v.tensor_copy(Z_bf[:, :], S_f[:, :]),
               waits=(("ve", att["ladd"]),))
        tap("Lloc", Z_bf[:, :])
        d_ccin = DMA_X(cc_in[:, :], Z_bf[:, :], sem="dCC",
                       waits=(("ve", ev),))
        ph3_pe = P.cnt("pe")
        ph3_ve = P.cnt("ve")
        barrier(extra_excl=("dX",))

    if _PHASE_LIMIT >= 4:
        ws["bufs"] = (wbA, wbB)
        ws["nb"] = 2
        ws["i"] = 0
        ws["consumed"] = {}
        ws["hold"] = (("pe", ph2_pe),)
        rot_reset()
        proj_fm(wq_h, lnb, qfm, bq_s)
        ws["hold"] = ()
        ph4_pe = P.cnt("pe")
        tap("qfm", qfm[:, :])
        DMA_W(dq_t[:, :], dq_c[:, :], sem="dAT",
              waits=(("pe", ph3_pe), ("ve", ph3_ve)))
        d_attc = DMA_W(dm_t[:, :], dm_c[:, :], sem="dAT")
    if _PHASE_LIMIT >= 3:
        P.emit("gpsimd",
               lambda g: g.collective_compute(
                   "AllGather", ALU.bypass,
                   replica_groups=[[0, 1, 2, 3, 4, 5, 6, 7]],
                   ins=[cc_in[:, :]], outs=[cc_out[:, :]]),
               waits=(("dCC", d_ccin),), inc="cc", inc_n=1)
        barrier()

    if _PHASE_LIMIT >= 6:
        V(lambda v: v.memset(S_f[:, :], 0.0))
        att["tr_done"] = att["tr_evac"] = att["sc_evac"] = 0
        att["inter"] = att["ao_evac"] = att["aoA"] = att["aoB"] = 0
        for c in range(NCH):
            chunk_local(c, first_dw=d_attc if c == 0 else None,
                        first_dx=d_ccin)

    comb_end = 0
    if _PHASE_LIMIT >= 5:
        Lbs = [wbA[:, 0:2048], wbB[:, 0:2048]]
        Cbs = [wbA[:, 2048:4096], wbB[:, 2048:4096]]
        z0 = V(lambda v: v.memset(sinit_bf[:, :], 0.0))
        prev = z0
        muls = {}
        adds = {}
        for j in range(8):
            p = j % 2
            sem = "dL5a" if p == 0 else "dL5b"
            w = [("cc", 1)]
            if j == 0:
                w.append(("pe", ph4_pe))
            if j >= 2:
                w.append(("ve", adds[j - 2]))
            dL = DMA_X(Lbs[p], cc_out[j * 128:(j + 1) * 128, :],
                       waits=tuple(w), sem=sem)
            dC = DMA_X(Cbs[p], coeff_h[j, :, :], sem=sem)
            muls[j] = V(lambda v, p_=p: v.tensor_tensor(Lbs[p_], Lbs[p_],
                                                        Cbs[p_], ALU.mult),
                        waits=((sem, dC),))
            prev = V(lambda v, p_=p: v.tensor_tensor(
                sinit_bf[:, :], sinit_bf[:, :], Lbs[p_], ALU.add),
                waits=(("ve", muls[j]), ("ve", prev)))
            adds[j] = prev
        comb_end = prev
        tap("Sinit", sinit_bf[:, :])

    if _PHASE_LIMIT >= 6:
        prev_mm = att["inter"]
        prev_ev = att["ao_evac"]
        for c in range(NCH):
            zc = 0
            for h in range(H):
                w = []
                if h == 0:
                    w.append(("ve", comb_end))
                    w.append(("pe", prev_mm))
                zc = A_(lambda s, h_=h, c_=c: s.activation(
                    Z_bf[:, h_ * 128:(h_ + 1) * 128],
                    sinit_bf[:, h_ * 128:(h_ + 1) * 128], AF.Identity,
                    scale=float(LAM[h_] ** (128.0 * c_))),
                    waits=tuple(w))
            mmc = 0
            for h in range(H):
                w = []
                if h == 0:
                    w.append(("ac", zc))
                    w.append(("ve", prev_ev))
                mmc = MM(p32[h // 4][:, (h % 4) * 128:(h % 4 + 1) * 128],
                         Z_bf[:, h * 128:(h + 1) * 128],
                         fmsl(qfm, h, c * 128, (c + 1) * 128),
                         True, True, waits=tuple(w))
            prev_mm = mmc
            for i in range(4):
                prev_ev = V(lambda v, i_=i, c_=c: v.tensor_tensor(
                    lnb.rearrange("p (b t) -> p b t", b=NB)
                    [:, 4 * i_:4 * i_ + 4, c_ * 128:(c_ + 1) * 128],
                    lnb.rearrange("p (b t) -> p b t", b=NB)
                    [:, 4 * i_:4 * i_ + 4, c_ * 128:(c_ + 1) * 128],
                    p32[i_][:, :].rearrange("p (h t) -> p h t", h=4),
                    ALU.add), waits=(("pe", mmc),) if i == 0 else ())
        tap("attn", lnb[:, :])
        barrier()

    if _PHASE_LIMIT >= 7:
        rot_reset()
        ws["hold"] = (("ve", comb_end),)
        blk7 = proj_fm(wo_h, lnb, qfm, bo_s)
        ws["hold"] = ()
        tap("yattn", qfm[:, :])

    if _PHASE_LIMIT >= 8:
        r = ln_prepare(qfm, sq_ac=True,
                       blk_ready={cb: ("ac", blk7[cb]) for cb in blk7})
        respo = {}
        for cb in range(NB):
            tb = outst[:, (cb % 2) * T:(cb % 2 + 1) * T]
            w = []
            if cb == 0:
                w.append(("ve", r))
            if cb >= 2:
                w.append(("po", respo[cb - 2]))
            V(lambda v, c=cb, o=tb: v.tensor_tensor(
                o, fmsl(qfm, c, 0, T), rb_s[:, :], ALU.mult), waits=tuple(w))
            dv = V(lambda v, o=tb: v.tensor_tensor(o, o, ab_s[:, :],
                                                   ALU.subtract))
            respo[cb] = PO(lambda g, c=cb, o=tb: g.tensor_tensor(
                fmsl(xbf, c, 0, T), fmsl(xbf, c, 0, T), o, ALU.add),
                waits=(("ve", dv),))
        tap("x2", xbf[:, :])

    if _PHASE_LIMIT >= 9:
        r = ln_prepare(xbf, sq_ac=True,
                       blk_ready={cb: ("po", respo[cb]) for cb in respo})
        ln2blk = ln_apply_to(xbf, lnb, r)
        tap("ln2", lnb[:, :])

    if _PHASE_LIMIT >= 10:
        sg_prev = None
        ev_half = 0
        we_done = 0
        for tc in range(NCH):
            last = 0
            for k in range(16):
                w = []
                if tc == 0:
                    w.append(ln2blk[k])
                if k == 0:
                    if tc == 0:
                        w.append(("dC1", d_gatew))
                    if sg_prev is not None:
                        w.append(("ac", sg_prev))
                last = MM(pb6[:, 0:8], fmsl(lnb, k, tc * 128, (tc + 1) * 128),
                          gatew_s[:, k * 8:(k + 1) * 8], k == 0, k == 15,
                          waits=tuple(w))
            sg = A_(lambda s, t_=tc: s.activation(
                gsig_s[:, t_ * 8:(t_ + 1) * 8], pb6[:, 0:8], AF.Sigmoid),
                waits=(("pe", last),))
            sg_prev = sg
            sig = gsig_s[:, tc * 8:(tc + 1) * 8]
            e = V(lambda v, s_=sig: v.tensor_tensor(g1_s[:, :], s_,
                                                    gbias_s[:, :], ALU.add),
                  waits=(("ac", sg),))
            e = V(lambda v: v.tensor_reduce(g3_s[:, :], g1_s[:, :], AX.X,
                                            ALU.max), waits=(("ve", e),))
            e = V(lambda v: v.tensor_scalar(g2_s[:, :], g1_s[:, :],
                                            g3_s[:, :], None, ALU.is_ge),
                  waits=(("ve", e),))
            e = V(lambda v: v.scalar_tensor_tensor(
                g1_s[:, :], g2_s[:, :], BIGNEG, g1_s[:, :], ALU.mult, ALU.add),
                waits=(("ve", e),))
            e = V(lambda v: v.tensor_reduce(g4_s[:, :], g1_s[:, :], AX.X,
                                            ALU.max), waits=(("ve", e),))
            e = V(lambda v: v.tensor_scalar(g1_s[:, :], g1_s[:, :],
                                            g4_s[:, :], None, ALU.is_ge),
                  waits=(("ve", e),))
            e = V(lambda v: v.tensor_tensor(g2_s[:, :], g2_s[:, :],
                                            g1_s[:, :], ALU.add),
                  waits=(("ve", e),))
            e = V(lambda v, s_=sig: v.tensor_tensor(g1_s[:, :], s_,
                                                    g2_s[:, :], ALU.mult),
                  waits=(("ve", e),))
            e = V(lambda v: v.tensor_reduce(g3_s[:, :], g1_s[:, :], AX.X,
                                            ALU.add), waits=(("ve", e),))
            e = V(lambda v: v.reciprocal(g4_s[:, :], g3_s[:, :]),
                  waits=(("ve", e),))
            e = V(lambda v, t_=tc: v.tensor_scalar(
                wn_s[:, t_ * 8:(t_ + 1) * 8], g1_s[:, :], g4_s[:, :], None,
                ALU.mult), waits=(("ve", e),))
            w = [("ve", e)]
            if tc == 4:
                w.append(("ve", ev_half))
            tr_cnt = MM(pb7[0:8, (tc % 4) * 128:(tc % 4 + 1) * 128],
                        wn_s[:, tc * 8:(tc + 1) * 8], ident_s[:, :],
                        True, True, waits=tuple(w))
            if tc == 3:
                ev_half = V(lambda v: v.tensor_copy(we_fm[:, 0:512],
                                                    pb7[0:8, :]),
                            waits=(("pe", tr_cnt),))
            if tc == 7:
                we_done = V(lambda v: v.tensor_copy(we_fm[:, 512:1024],
                                                    pb7[0:8, :]),
                            waits=(("pe", tr_cnt),))
        tap("wefm", we_fm[:, :])

    def up_pairs(tiles, nblk, we_par, kwaits=None, we_wait=None,
                 mid_hook=None, late_hook=None):
        hv_prev = {0: None, 1: None}
        po_last = 0
        po_blk = {}
        hv_blk = {}
        first_po = [we_wait]

        def consume(j, idx, dtar):
            nonlocal po_last
            if mid_hook is not None and j == nblk // 2:
                mid_hook()
            if late_hook is not None and j >= nblk - 2:
                late_hook()
            for tt in range(TT):
                bg = next_bank()
                bu = next_bank()
                evg = rot["evac"][bg]
                evu = rot["evac"][bu]
                for k in range(16):
                    w = []
                    if j == 0 and tt == 0 and kwaits is not None:
                        w.append(kwaits[k])
                    if k == 0:
                        w.append(dtar)
                        if evg is not None:
                            w.append(evg)
                    MM(p32[bg][:, :], wbuf(idx)[:, k * 256:k * 256 + 128],
                       fmsl(lnb, k, tt * TN, (tt + 1) * TN),
                       k == 0, k == 15, waits=tuple(w))
                for k in range(16):
                    w = []
                    if k == 0 and evu is not None:
                        w.append(evu)
                    MM(p32[bu][:, :],
                       wbuf(idx)[:, k * 256 + 128:k * 256 + 256],
                       fmsl(lnb, k, tt * TN, (tt + 1) * TN),
                       k == 0, k == 15, waits=tuple(w))
                pe_c = P.cnt("pe")
                tmp = tmpA if tt == 0 else tmpB
                w = [("pe", pe_c)]
                if hv_prev[tt] is not None:
                    w.append(("ve", hv_prev[tt]))
                sgc = A_(lambda s, b_=bg, t_=tmp: s.activation(
                    t_[:, 0:TN], p32[b_][:, :], AF.Silu), waits=tuple(w))
                rot["evac"][bg] = ("ac", sgc)
                hv = V(lambda v, b_=bu, j_=j, t_=tt, tm=tmp: v.tensor_tensor(
                    hbuf[:, j_ * T + t_ * TN: j_ * T + (t_ + 1) * TN],
                    tm[:, 0:TN], p32[b_][:, :], ALU.mult),
                    waits=(("ac", sgc),))
                rot["evac"][bu] = ("ve", hv)
                hv_prev[tt] = hv
                hv_blk[j] = hv
                if we_par is not None:
                    wpo = [("ve", hv)]
                    if first_po[0] is not None:
                        wpo.append(first_po[0])
                        first_po[0] = None
                    po_last = PO(lambda g, j_=j, t_=tt: g.tensor_tensor(
                        hbuf[:, j_ * T + t_ * TN: j_ * T + (t_ + 1) * TN],
                        hbuf[:, j_ * T + t_ * TN: j_ * T + (t_ + 1) * TN],
                        we_b[we_par][:, t_ * TN:(t_ + 1) * TN], ALU.mult),
                        waits=tuple(wpo))
                    po_blk[j] = po_last
        stream(tiles, consume)
        return P.cnt("ve"), po_last, (po_blk if we_par is not None else hv_blk)

    def down_proj(tiles, nk, fpt, first_write, h_sem, h_blk, pre=None):
        blk_ve = {}

        def consume(j, idx, dtar):
            for fc in range(fpt):
                blk = j * fpt + fc
                for tt in range(TT):
                    b = next_bank()
                    ev = rot["evac"][b]
                    for k in range(nk):
                        w = []
                        if k == 0:
                            w.append(dtar)
                            if ev is not None:
                                w.append(ev)
                        if j == 0 and fc == 0 and tt == 0 and k in h_blk:
                            w.append((h_sem, h_blk[k]))
                        MM(p32[b][:, :],
                           wbuf(idx)[:, k * (128 * fpt) + fc * 128:
                                       k * (128 * fpt) + fc * 128 + 128],
                           hbuf[:, k * T + tt * TN: k * T + (tt + 1) * TN],
                           k == 0, k == nk - 1, waits=tuple(w))
                    pe_c = P.cnt("pe")
                    if first_write:
                        cnt = V(lambda v, b_=b, bl=blk, t_=tt: v.tensor_copy(
                            fmsl(acc, bl, t_ * TN, (t_ + 1) * TN),
                            p32[b_][:, :]), waits=(("pe", pe_c),))
                    else:
                        cnt = V(lambda v, b_=b, bl=blk, t_=tt: v.tensor_tensor(
                            fmsl(acc, bl, t_ * TN, (t_ + 1) * TN),
                            fmsl(acc, bl, t_ * TN, (t_ + 1) * TN),
                            p32[b_][:, :], ALU.add), waits=(("pe", pe_c),))
                    rot["evac"][b] = ("ve", cnt)
                    blk_ve[blk] = cnt
        stream(tiles, consume, pre=pre)
        return blk_ve

    if _PHASE_LIMIT >= 11:
        ws["bufs"] = (wbA, wbB, wb3)
        ws["nb"] = 3
        ws["i"] = 0
        ws["consumed"] = {}
        ws["hold"] = (("pe", P.cnt("pe")), ("ve", P.cnt("ve")))
        rot_reset()
        gw_tiles = [(gw_h[j, :, :, :].rearrange("k p f -> p k f"), 16 * 256)
                    for j in range(22)]
        dw_tiles = [(dwt_h[j, :, :, :].rearrange("k p f -> p k f"), 22 * 128)
                    for j in range(16)]
        dw_pre = []

        def lh11():
            dw_pre.append(wload(*dw_tiles[len(dw_pre)]))
        hr, _, hblk = up_pairs(gw_tiles, 22, None, kwaits=ln2blk,
                               late_hook=lh11)
        ws["hold"] = ()
        rot_reset()
        down_proj(dw_tiles, 22, 1, True, "ve", hblk, pre=dw_pre)
        tap("shared", acc[:, :])

    if _PHASE_LIMIT >= 12:
        ws["bufs"] = (wbA, wbB, wb3, wb4)
        ws["nb"] = 4
        ws["i"] = 0
        ws["consumed"] = {}
        ws["hold"] = (("pe", P.cnt("pe")),)
        mb_prev = 0
        wready = {}
        po_hist = {}

        def we_prep(e):
            nonlocal mb_prev
            par = e % 2
            w = [("ve", we_done)]
            if mb_prev:
                w.append(("pe", mb_prev))
            if e >= 2 and (e - 2) in po_hist:
                w.append(("po", po_hist[e - 2]))
            dwe = DMA_X(we_row[:, :], we_fm[e:e + 1, :], waits=tuple(w),
                        sem="dWE")
            wmb = [("dWE", dwe)]
            if (e - 1) in wready:
                wmb.append(("ve", wready[e - 1]))
            mb1 = MM(pb6[:, :], onesrow_s[:, :], we_row[0:1, 0:TN],
                     True, True, waits=tuple(wmb))
            mb2 = MM(pb7[:, :], onesrow_s[:, :], we_row[0:1, TN:2 * TN],
                     True, True)
            mb_prev = mb2
            V(lambda v, p_=par: v.tensor_copy(we_b[p_][:, 0:TN], pb6[:, :]),
              waits=(("pe", mb1),))
            wready[e] = V(lambda v, p_=par: v.tensor_copy(
                we_b[p_][:, TN:2 * TN], pb7[:, :]), waits=(("pe", mb2),))

        we_prep(0)
        dn_blk = {}
        for e in range(NE):
            par = e % 2
            rot_reset()
            up_tiles = [(up_h[e, j, :, :, :].rearrange("k p f -> p k f"),
                         16 * 256) for j in range(11)]
            dn_tiles = [(dn_h[e, j, :, :, :].rearrange("k p f -> p k f"),
                         11 * 256) for j in range(8)]
            hook = (lambda e_=e: we_prep(e_ + 1)) if e + 1 < NE else None
            dn_pre = []

            def lhe(dt_=dn_tiles, dp_=dn_pre):
                dp_.append(wload(*dt_[len(dp_)]))
            hr, hpo, hblk = up_pairs(up_tiles, 11, par,
                                     we_wait=("ve", wready[e]),
                                     mid_hook=hook, late_hook=lhe)
            if e == 0:
                ws["hold"] = ()
            po_hist[e] = hpo
            rot_reset()
            dn_blk = down_proj(dn_tiles, 11, 2, False, "po", hblk,
                               pre=dn_pre)
        tap("ffn", acc[:, :])

    if _PHASE_LIMIT >= 13:
        r = ln_prepare(acc, sq_ac=True,
                       blk_ready={cb: ("ve", dn_blk[cb]) for cb in dn_blk})
        dds = {}
        first_po13 = True
        for cb in range(NB):
            ost = outst[:, (cb % 2) * T:(cb % 2 + 1) * T]
            w = []
            if cb == 0:
                w.append(("ve", r))
            osem = "dOa" if cb % 2 == 0 else "dOb"
            if cb >= 2:
                w.append((osem, dds[cb - 2]))
            if cb % 2 == 0:
                V(lambda v, c=cb, o=ost: v.tensor_tensor(
                    o, fmsl(acc, c, 0, T), rb_s[:, :], ALU.mult),
                  waits=tuple(w))
                V(lambda v, o=ost: v.tensor_tensor(o, o, ab_s[:, :],
                                                   ALU.subtract))
                dv = V(lambda v, c=cb, o=ost: v.tensor_tensor(
                    o, o, fmsl(xbf, c, 0, T), ALU.add))
                dsem = "ve"
            else:
                if first_po13:
                    w.append(("ve", r))
                    first_po13 = False
                w.append(("ve", dn_blk[cb]))
                PO(lambda g, c=cb, o=ost: g.tensor_tensor(
                    o, fmsl(acc, c, 0, T), rb_s[:, :], ALU.mult),
                   waits=tuple(w))
                PO(lambda g, o=ost: g.tensor_tensor(o, o, ab_s[:, :],
                                                    ALU.subtract))
                dv = PO(lambda g, c=cb, o=ost: g.tensor_tensor(
                    o, o, fmsl(xbf, c, 0, T), ALU.add))
                dsem = "po"
            dds[cb] = DMA_X(y_h[cb * 128:(cb + 1) * 128, :], ost,
                            waits=((dsem, dv),), sem=osem)

    WAIT("sync", tuple((s, P.cnt(s)) for s in DMA_SEMS if P.cnt(s) > 0))
    WAIT("gpsimd", tuple((s, P.cnt(s))
                         for s in ("dWa", "dWb", "dWc", "dWd", "dWe", "dWf")
                         if P.cnt(s) > 0))

    from contextlib import ExitStack
    with ExitStack() as ctx:
        sems = {s: ctx.enter_context(nc.semaphore(s)) for s in SEMS}
        block = ctx.enter_context(nc.Block())
        for eng_name in ENGS:
            def mkbody(en):
                def body(eng):
                    for (builder, waits, inc, inc_n) in P.ops[en]:
                        for (s, v) in waits:
                            eng.wait_ge(sems[s], v)
                        if builder is not None:
                            inst = builder(eng)
                            if inc is not None:
                                inst.then_inc(sems[inc], inc_n)
                return body
            getattr(block, eng_name)(mkbody(eng_name))
    return nc


def _prep_weights(w_qkv, b_qkv, w_out, b_out, gate_w, gate_bias, shared_gw,
                  shared_dw, up_w, down_w):
    C = np.ascontiguousarray

    def fm_tiles(WT):
        F = WT.shape[1]
        return C(WT.reshape(16, 128, F // 256, 256).transpose(2, 0, 1, 3))

    d = {}
    d["wq"] = fm_tiles(w_qkv[0:2048].T)
    d["wk"] = fm_tiles(w_qkv[2048:4096].T)
    d["wv"] = fm_tiles(w_qkv[4096:6144].T)
    d["wo"] = fm_tiles(w_out.T)
    d["bq"] = C(b_qkv[0:2048].reshape(16, 128).T)
    d["bk"] = C(b_qkv[2048:4096].reshape(16, 128).T)
    d["bv"] = C(b_qkv[4096:6144].reshape(1, 2048))
    d["bo"] = C(b_out.reshape(16, 128).T)
    A = np.asarray(shared_gw).T
    y = A[:, 0:2816].reshape(16, 128, 22, 128)
    g = A[:, 2816:5632].reshape(16, 128, 22, 128)
    gwt = np.empty((22, 16, 128, 256), np.float32)
    gwt[..., 0:128] = g.transpose(2, 0, 1, 3)
    gwt[..., 128:256] = y.transpose(2, 0, 1, 3)
    d["gw"] = C(gwt)
    dwT = np.asarray(shared_dw).T
    d["dwt"] = C(dwT.reshape(22, 128, 16, 128).transpose(2, 0, 1, 3))
    up = np.empty((NE, 11, 16, 128, 256), np.float32)
    dn = np.empty((NE, 8, 11, 128, 256), np.float32)
    for e in range(NE):
        UT = np.asarray(up_w[e]).T
        ge = UT[:, 0:1408].reshape(16, 128, 11, 128)
        ue = UT[:, 1408:2816].reshape(16, 128, 11, 128)
        up[e, ..., 0:128] = ge.transpose(2, 0, 1, 3)
        up[e, ..., 128:256] = ue.transpose(2, 0, 1, 3)
        DT = np.asarray(down_w[e]).T
        dn[e] = DT.reshape(11, 128, 8, 256).transpose(2, 0, 1, 3)
    d["up"] = C(up)
    d["dn"] = C(dn)
    gT = np.asarray(gate_w).T.reshape(16, 128, 8)
    d["gatew"] = C(gT.transpose(1, 0, 2))
    d["gbias"] = C(np.broadcast_to(np.asarray(gate_bias)[None, :],
                                   (128, 8)).astype(np.float32))
    return d


def kernel(L0_attn_x, L0_attn_w_qkv, L0_attn_b_qkv, L0_attn_w_out,
           L0_attn_b_out, L0_ffn_gate_w, L0_ffn_gate_bias, L0_ffn_shared_gw,
           L0_ffn_shared_dw, L0_ffn_up_w, L0_ffn_down_w):
    C = np.ascontiguousarray
    args = [np.asarray(a) for a in
            (L0_attn_x, L0_attn_w_qkv, L0_attn_b_qkv, L0_attn_w_out,
             L0_attn_b_out, L0_ffn_gate_w, L0_ffn_gate_bias, L0_ffn_shared_gw,
             L0_ffn_shared_dw, L0_ffn_up_w, L0_ffn_down_w)]
    x = args[0]
    wd = _prep_weights(*args[1:])
    nc = build_program()
    in_maps = []
    for core in range(8):
        b, seg = core // 4, core % 4
        xT = C(x[b, seg * T:(seg + 1) * T, :].T.reshape(NB, 128, T))
        coeff = np.zeros((8, 128, 2048), np.float64)
        for j in range(8):
            bj, segj = j // 4, j % 4
            if bj == b and segj < seg:
                for h in range(H):
                    coeff[j, :, h * 128:(h + 1) * 128] = \
                        LAM[h] ** (1024.0 * (seg - segj - 1))
        m = dict(wd)
        m["xT"] = xT
        import ml_dtypes as _mld
        m["coeff"] = coeff.astype(_mld.bfloat16)
        in_maps.append(m)
    res = run_bass_kernel_spmd(nc, in_maps, core_ids=list(range(8)))
    globals()['_LAST_RES'] = res
    out = np.empty((B, N, D), np.float32)
    for core in range(8):
        b, seg = core // 4, core % 4
        out[b, seg * T:(seg + 1) * T, :] = res.results[core]["y"].T
    return out


# version 22
# speedup vs baseline: 1.0140x; 1.0140x over previous
import math
import numpy as np
import concourse.bass as bass
import concourse.mybir as mybir
from concourse.bass_utils import run_bass_kernel_spmd

F32 = mybir.dt.float32
BF16 = mybir.dt.bfloat16
AF = mybir.ActivationFunctionType
ALU = mybir.AluOpType
AX = mybir.AxisListType

B, N, D = 2, 4096, 2048
H = 16
T = 1024
NCH = 8
NB = 16
NE = 8
TT = 2
TN = 512
EPS = 1e-5
BIGNEG = -1e30

ENGS = ("tensor", "vector", "scalar", "gpsimd", "sync")
SEMS = ("dWa", "dWb", "dWc", "dWd", "dI", "dC1", "dAT", "dBV", "dT8", "dCC",
        "dL5a", "dL5b", "dWE", "dOa", "dOb", "dX", "pe", "ve", "ac",
        "po", "cc")
DMA_SEMS = ("dWa", "dWb", "dWc", "dWd", "dI", "dC1", "dAT", "dBV", "dT8",
            "dCC", "dL5a", "dL5b", "dWE", "dOa", "dOb", "dX")
OWN = {"tensor": ("pe",), "vector": ("ve",), "scalar": ("ac",),
       "gpsimd": ("po",) + DMA_SEMS, "sync": DMA_SEMS}

_TAPS = ()
_PHASE_LIMIT = 99


def _slopes(n):
    start = 2.0 ** (-2.0 ** (-(math.log2(n) - 3)))
    return np.array([start ** (i + 1) for i in range(n)], dtype=np.float64)


LAM = np.exp(-_slopes(H))


class Plan:
    def __init__(self):
        self.ops = {e: [] for e in ENGS}
        self.ctr = {s: 0 for s in SEMS}

    def cnt(self, s):
        return self.ctr[s]

    def emit(self, eng, builder, waits=(), inc=None, inc_n=1):
        if inc is not None:
            self.ctr[inc] += inc_n
        self.ops[eng].append((builder, tuple(waits), inc, inc_n))
        return self.ctr[inc] if inc else None

    def gate(self):
        return dict(self.ctr)


def build_program():
    nc = bass.Bass("TRN2", target_bir_lowering=False)

    xT_h = nc.dram_tensor("xT", [NB, 128, T], F32, kind="ExternalInput")
    wk_h = nc.dram_tensor("wk", [8, 16, 128, 256], F32, kind="ExternalInput")
    wv_h = nc.dram_tensor("wv", [8, 16, 128, 256], F32, kind="ExternalInput")
    wq_h = nc.dram_tensor("wq", [8, 16, 128, 256], F32, kind="ExternalInput")
    wo_h = nc.dram_tensor("wo", [8, 16, 128, 256], F32, kind="ExternalInput")
    bq_h = nc.dram_tensor("bq", [128, 16], F32, kind="ExternalInput")
    bk_h = nc.dram_tensor("bk", [128, 16], F32, kind="ExternalInput")
    bv_h = nc.dram_tensor("bv", [1, 2048], F32, kind="ExternalInput")
    bo_h = nc.dram_tensor("bo", [128, 16], F32, kind="ExternalInput")
    gw_h = nc.dram_tensor("gw", [22, 16, 128, 256], F32, kind="ExternalInput")
    dwt_h = nc.dram_tensor("dwt", [16, 22, 128, 128], F32, kind="ExternalInput")
    up_h = nc.dram_tensor("up", [NE, 11, 16, 128, 256], F32, kind="ExternalInput")
    dn_h = nc.dram_tensor("dn", [NE, 8, 11, 128, 256], F32, kind="ExternalInput")
    gate_h = nc.dram_tensor("gatew", [128, 16, 8], F32, kind="ExternalInput")
    gbias_h = nc.dram_tensor("gbias", [128, 8], F32, kind="ExternalInput")
    coeff_h = nc.dram_tensor("coeff", [8, 128, 2048], BF16, kind="ExternalInput")
    y_h = nc.dram_tensor("y", [D, T], F32, kind="ExternalOutput")

    cc_in = nc.dram_tensor("cc_in", [128, 2048], BF16, kind="Internal")
    cc_out = nc.dram_tensor("cc_out", [1024, 2048], BF16, kind="Internal",
                            addr_space="Shared")

    t_i = np.arange(128, dtype=np.float64)
    dq_np = np.zeros((128, H * 128), np.float64)
    dk_np = np.zeros((128, H * 128), np.float64)
    dm_np = np.zeros((128, H * 128), np.float64)
    l128_np = np.zeros((128, H * 128), np.float64)
    jj, ii = t_i[:, None], t_i[None, :]
    for h in range(H):
        dq_np[:, h * 128:(h + 1) * 128] = (LAM[h] ** (t_i + 1.0))[None, :]
        dk_np[:, h * 128:(h + 1) * 128] = (LAM[h] ** (127.0 - t_i))[None, :]
        l128_np[:, h * 128:(h + 1) * 128] = LAM[h] ** 128.0
        dm_np[:, h * 128:(h + 1) * 128] = np.where(ii >= jj, LAM[h] ** (ii - jj), 0.0)
    dkcol_np = np.zeros((128, 16), np.float64)
    dkc8_np = np.zeros((128, 8 * 16), np.float64)
    for h in range(H):
        dkcol_np[:, h] = LAM[h] ** (127.0 - t_i)
        for c in range(8):
            dkc8_np[:, c * 16 + h] = LAM[h] ** (127.0 - t_i + 128.0 * (7 - c))
    dq_c = nc.inline_tensor(dq_np.astype(np.float32), name="dq_c")
    dm_c = nc.inline_tensor(dm_np.astype(np.float32), name="dm_c")
    dkcol_c = nc.inline_tensor(dkcol_np.astype(np.float32), name="dkcol_c")
    dkc8_c = nc.inline_tensor(dkc8_np.astype(np.float32), name="dkc8_c")
    dkc8t_np = np.zeros((8, 128, H * 128), np.float64)
    for h in range(H):
        for c in range(8):
            dkc8t_np[c, :, h * 128:(h + 1) * 128] = \
                (LAM[h] ** (127.0 - t_i + 128.0 * (7 - c)))[:, None]
    dkc8t_c = nc.inline_tensor(dkc8t_np.astype(np.float32), name="dkc8t_c")
    ident_c = nc.inline_tensor(np.eye(128, dtype=np.float32), name="ident_c")
    onescol_c = nc.inline_tensor(np.ones((128, 1), np.float32), name="onescol_c")
    onesrow_c = nc.inline_tensor(np.ones((1, 128), np.float32), name="onesrow_c")
    eps_c = nc.inline_tensor(np.full((1, 1), EPS, np.float32), name="eps_c")

    KB = 1024
    BO = (nc.sbuf_base + 31) // 32 * 32

    def sb(name, off, shape, dtype):
        import concourse.mybir as _mb
        end = BO + off + shape[1] * _mb.dt.size(dtype)
        assert end <= 229344, (name, off, end)
        return nc.alloc_sbuf_tensor_at(name, shape, dtype, offset=BO + off)

    xbf = sb("xbf", 0 * KB, [128, NB * T], BF16)
    lnb = sb("lnb", 32 * KB, [128, NB * T], BF16)
    qfm = sb("qfm", 64 * KB, [128, NB * T], BF16)
    kfm = sb("kfm", 96 * KB, [128, NB * T], BF16)
    vtm = sb("vtm", 128 * KB, [128, NCH * D], BF16)
    acc = sb("acc", 64 * KB, [128, NB * T], BF16)
    hbuf = sb("hbuf", 128 * KB, [128, 22 * T], BF16)
    we_fm = sb("we_fm", 96 * KB, [8, T], BF16)
    we_row = sb("we_row", 100 * KB, [1, T], BF16)
    we_b = [sb("we_b0", 104 * KB, [128, T], BF16),
            sb("we_b1", 108 * KB, [128, T], BF16)]
    outst = sb("outst", 112 * KB, [128, 2 * T], F32)
    ktm_s = sb("ktm_s", 160 * KB, [128, H * 128], BF16)
    Z_bf = sb("Z_bf", 164 * KB, [128, H * 128], BF16)
    dq_t = sb("dq_t", 168 * KB, [128, H * 128], BF16)
    dkc8b = sb("dkc8b", 168 * KB, [128, H * 128], BF16)
    dm_t = sb("dm_t", 172 * KB, [128, H * 128], BF16)
    S_f = sb("S_f", 176 * KB, [128, H * 128], F32)
    rb_s = sb("rb_s", 160 * KB, [128, T], BF16)
    ab_s = sb("ab_s", 164 * KB, [128, T], BF16)
    rows = sb("rows", 168 * KB, [1, 3 * T], F32)
    rows_bf = sb("rows_bf", 180 * KB, [1, 2 * T], BF16)
    Lbuf = sb("Lbuf", 160 * KB, [128, H * 128], F32)
    cbuf = sb("cbuf", 168 * KB, [128, H * 128], BF16)
    bv_s = sb("bv_s", 168 * KB, [1, 2048], BF16)
    M0 = 184 * KB
    bq_s = sb("bq_s", M0 + 0, [128, 16], F32)
    bk_s = sb("bk_s", M0 + 64, [128, 16], F32)
    bo_s = sb("bo_s", M0 + 128, [128, 16], F32)
    ident_s = sb("ident_s", M0 + 192, [128, 128], BF16)
    onescol_s = sb("onescol_s", M0 + 448, [128, 1], BF16)
    onesrow_s = sb("onesrow_s", M0 + 480, [1, 128], BF16)
    onesrowf_s = sb("onesrowf_s", M0 + 736, [1, 128], F32)
    gbias_s = sb("gbias_s", M0 + 1248, [128, 8], F32)
    gatew_s = sb("gatew_s", M0 + 1280, [128, 16 * 8], BF16)
    gsig_s = sb("gsig_s", M0 + 1536, [128, 64], F32)
    wn_s = sb("wn_s", M0 + 1792, [128, 64], BF16)
    g1_s = sb("g1_s", M0 + 1920, [128, 8], F32)
    g2_s = sb("g2_s", M0 + 1952, [128, 8], F32)
    g3_s = sb("g3_s", M0 + 1984, [128, 1], F32)
    g4_s = sb("g4_s", M0 + 2016, [128, 1], F32)
    eps_s = sb("eps_s", M0 + 2048, [1, 1], F32)
    dkcol_s = sb("dkcol_s", M0 + 2080, [128, 16], F32)
    dkc8_s = sb("dkc8_s", 206 * KB + 512, [128, 8 * 16], F32)
    sinit_bf = sb("sinit_bf", 186 * KB + 512, [128, H * 128], BF16)
    tmpA = sb("tmpA", 186 * KB + 512, [128, T], BF16)
    tmpB = sb("tmpB", 188 * KB + 512, [128, T], BF16)
    wbA = sb("wbA", 190 * KB + 512, [128, 16 * 256], BF16)
    wbB = sb("wbB", 198 * KB + 512, [128, 16 * 256], BF16)
    wb3 = sb("wb3", 120 * KB, [128, 16 * 256], BF16)
    wb4 = sb("wb4", 152 * KB, [128, 16 * 256], BF16)
    wbQ = sb("wbQ", 64 * KB, [128, 16 * 256], BF16)
    wb = (wbA, wbB, wbQ)

    p32 = [nc.alloc_psum_tensor(f"p32_{i}", [128, 512], F32) for i in range(4)]
    trA = nc.alloc_psum_tensor("trA", [128, 1024], BF16)
    trB = nc.alloc_psum_tensor("trB", [128, 1024], BF16)
    pb6 = nc.alloc_psum_tensor("pb6", [128, 512], F32)
    pb7 = nc.alloc_psum_tensor("pb7", [128, 512], F32)

    P = Plan()

    def MM(out, lhsT, rhs, start, stop, waits=()):
        P.emit("tensor",
               lambda t, o=out, l=lhsT, r=rhs, s0=start, s1=stop:
               t.matmul(o, l, r, start=s0, stop=s1), waits=waits, inc="pe")
        return P.cnt("pe")

    def TRP(out, in_, waits=()):
        P.emit("tensor",
               lambda t, o=out, i=in_: t.transpose(o, i, ident_s[:, :]),
               waits=waits, inc="pe")
        return P.cnt("pe")

    def V(builder, waits=()):
        P.emit("vector", builder, waits=waits, inc="ve")
        return P.cnt("ve")

    def A_(builder, waits=()):
        P.emit("scalar", builder, waits=waits, inc="ac")
        return P.cnt("ac")

    def PO(builder, waits=()):
        P.emit("gpsimd", builder, waits=waits, inc="po")
        return P.cnt("po")

    def DMA_W(out, in_, waits=(), sem="dC1"):
        P.emit("gpsimd", lambda g, o=out, i=in_: g.dma_start(out=o, in_=i),
               waits=waits, inc=sem, inc_n=16)
        return P.cnt(sem)

    def DMA_X(out, in_, waits=(), sem="dX"):
        P.emit("sync", lambda s, o=out, i=in_: s.dma_start(out=o, in_=i),
               waits=waits, inc=sem, inc_n=16)
        return P.cnt(sem)

    def WAIT(eng, waits):
        if waits:
            P.emit(eng, None, waits=waits)

    def barrier(extra_excl=()):
        g = P.gate()
        for e in ENGS:
            excl = set(OWN[e]) | {"dWa", "dWb", "dWc", "dWd", "cc"} | set(extra_excl)
            w = tuple((s, v) for s, v in g.items() if v > 0 and s not in excl)
            WAIT(e, w)

    def tap(name, ap):
        if name in _TAPS:
            th = nc.dram_tensor("tap_" + name, list(ap.shape), ap.tensor.dtype,
                                kind="ExternalOutput")
            g = P.gate()
            w = tuple((s, v) for s, v in g.items() if v > 0 and s != "dX")
            DMA_X(th[tuple(slice(0, s) for s in ap.shape)], ap, waits=w)

    def fmsl(tsr, blk, t0, t1):
        return tsr[:, blk * T + t0: blk * T + t1]

    def blk3(tsr, c):
        return tsr.rearrange("p (b t) -> p b t", b=NB)[:, :, c * 128:(c + 1) * 128]

    def hview(tsr):
        return tsr.rearrange("p (h t) -> p h t", h=H)

    ws = {"i": 0, "consumed": {}, "bufs": None, "nb": 3}

    def wbuf(i):
        return (ws["bufs"] or wb)[i % ws["nb"]]

    WSEMS = ("dWa", "dWb", "dWc", "dWd")

    def wload(src_ap, nkf):
        i = ws["i"]
        ws["i"] += 1
        nb = ws["nb"]
        waits = (("pe", ws["consumed"][i - nb]),) if i >= nb else ()
        waits = waits + tuple(ws.get("hold", ()))
        dst = wbuf(i)[:, :nkf].rearrange("p (k f) -> p k f", k=src_ap.shape[0])
        sem = WSEMS[i % nb]
        P.emit("gpsimd", lambda g, o=dst, s=src_ap: g.dma_start(out=o, in_=s),
               waits=waits, inc=sem, inc_n=16)
        return i, (sem, P.cnt(sem))

    def stream(tiles, consume, pre=None):
        nb = ws["nb"]
        pend = list(pre) if pre else []
        pend += [wload(*t) for t in tiles[len(pend):nb]]
        for j in range(len(tiles)):
            idx, dtar = pend[j]
            consume(j, idx, dtar)
            ws["consumed"][idx] = P.cnt("pe")
            if j + nb < len(tiles):
                pend.append(wload(*tiles[j + nb]))

    rot = {"r": 0, "evac": [None, None, None, None]}

    def rot_reset():
        rot["r"] = 0
        g = ("ve", P.cnt("ve")) if P.cnt("ve") else None
        rot["evac"] = [g, g, g, g]

    def next_bank():
        b = rot["r"] % 4
        rot["r"] += 1
        return b

    DMA_W(ident_s[:, :], ident_c[:, :])
    DMA_W(onescol_s[:, :], onescol_c[:, :])
    DMA_W(onesrow_s[:, :], onesrow_c[:, :])
    d_gatew = DMA_W(gatew_s[:, :], gate_h[:, :, :].rearrange("p k e -> p (k e)"))
    d_bfc = P.cnt("dC1")
    wk_tiles0 = [(wk_h[j, :, :, :].rearrange("k p f -> p k f"), 16 * 256)
                 for j in range(8)]
    wk_pre = []
    d_xg = {}
    for g4 in range(4):
        d4 = DMA_W(xbf[:, g4 * 4 * T:(g4 + 1) * 4 * T]
                   .rearrange("p (b t) -> p b t", b=4),
                   xT_h[g4 * 4:(g4 + 1) * 4, :, :].rearrange("b p t -> p b t"),
                   sem="dI")
        for cb in range(g4 * 4, g4 * 4 + 4):
            d_xg[cb] = ("dI", d4)
    for t_ in range(3):
        wk_pre.append(wload(*wk_tiles0[t_]))
    DMA_X(bq_s[:, :], bq_h[:, :])
    DMA_X(bk_s[:, :], bk_h[:, :])
    DMA_X(bo_s[:, :], bo_h[:, :])
    DMA_X(gbias_s[:, :], gbias_h[:, :])
    d_orow = DMA_X(onesrowf_s[:, :], onesrow_c[:, :])
    DMA_X(eps_s[:, :], eps_c[:, :])
    DMA_X(dkcol_s[:, :], dkcol_c[:, :])
    DMA_X(dkc8_s[:, :], dkc8_c[:, :])

    def ln_prepare(src, sq_ac=False, first_dw=None, blk_ready=None):
        mrow = rows[0:1, 0:T]
        vrow = rows[0:1, T:2 * T]
        trow = rows[0:1, 2 * T:3 * T]
        pe_start = P.cnt("pe")
        nmm = 0
        qi = 0
        sq_sem = "ac" if sq_ac else "ve"
        for tt in range(TT):
            for cb in range(NB):
                w = []
                if cb == 0 and tt == 0 and first_dw is not None:
                    w.append(first_dw)
                if blk_ready is not None:
                    w.append(blk_ready[cb])
                if qi >= 4:
                    w.append(("pe", pe_start + (qi - 4) * 2 + 2))
                half = qi % 4
                tS = (tmpA if half < 2 else tmpB)[:, (half % 2) * TN:
                                                  (half % 2 + 1) * TN]
                if sq_ac:
                    cS = A_(lambda s, c=cb, t_=tt, x_=tS: s.activation(
                        x_, fmsl(src, c, t_ * TN, (t_ + 1) * TN), AF.Square),
                        waits=tuple(w))
                else:
                    cS = V(lambda v, c=cb, t_=tt, x_=tS: v.tensor_tensor(
                        x_, fmsl(src, c, t_ * TN, (t_ + 1) * TN),
                        fmsl(src, c, t_ * TN, (t_ + 1) * TN), ALU.mult),
                        waits=tuple(w))
                w0 = []
                if cb == 0 and tt == 0:
                    w0.append(("dC1", d_bfc))
                    if first_dw is not None:
                        w0.append(first_dw)
                if blk_ready is not None:
                    w0.append(blk_ready[cb])
                prow = 0 if tt == 0 else 32
                MM(pb6[prow:prow + 1, :], onescol_s[:, :],
                   fmsl(src, cb, tt * TN, (tt + 1) * TN),
                   cb == 0, cb == NB - 1, waits=tuple(w0))
                MM(pb7[prow:prow + 1, :], onescol_s[:, :], tS,
                   cb == 0, cb == NB - 1, waits=((sq_sem, cS),))
                nmm += 2
                qi += 1
        pe_stats = P.cnt("pe")
        V(lambda v: v.tensor_copy(mrow[:, 0:TN], pb6[0:1, :]),
          waits=(("pe", pe_stats),))
        V(lambda v: v.tensor_copy(mrow[:, TN:2 * TN], pb6[32:33, :]))
        V(lambda v: v.tensor_copy(vrow[:, 0:TN], pb7[0:1, :]))
        V(lambda v: v.tensor_copy(vrow[:, TN:2 * TN], pb7[32:33, :]))
        V(lambda v: v.tensor_scalar(rows[0:1, 0:2 * T], rows[0:1, 0:2 * T],
                                    1.0 / D, None, ALU.mult))
        V(lambda v: v.tensor_tensor(trow[:, :], mrow[:, :], mrow[:, :],
                                    ALU.mult))
        e = V(lambda v: v.tensor_sub(vrow[:, :], vrow[:, :], trow[:, :]))
        a1 = A_(lambda s: s.activation(trow[:, :], vrow[:, :], AF.Sqrt,
                                       bias=eps_s[0:1, 0:1], scale=1.0),
                waits=(("ve", e),))
        er = V(lambda v: v.reciprocal(vrow[:, :], trow[:, :]),
               waits=(("ac", a1),))
        V(lambda v: v.tensor_tensor(trow[:, :], mrow[:, :], vrow[:, :],
                                    ALU.mult), waits=(("ve", er),))
        ec = V(lambda v: v.tensor_copy(rows_bf[:, :], rows[0:1, T:3 * T]))
        m1 = MM(pb6[:, :], onesrow_s[:, :], rows_bf[0:1, 0:TN], True, True,
                waits=(("ve", ec), ("dC1", d_bfc)))
        m2 = MM(pb7[:, :], onesrow_s[:, :], rows_bf[0:1, TN:2 * TN],
                True, True, waits=(("ve", ec),))
        b1 = V(lambda v: v.tensor_copy(rb_s[:, 0:TN], pb6[:, :]),
               waits=(("pe", m1),))
        b2 = V(lambda v: v.tensor_copy(rb_s[:, TN:2 * TN], pb7[:, :]),
               waits=(("pe", m2),))
        m3 = MM(pb6[:, :], onesrow_s[:, :], rows_bf[0:1, 2 * TN:3 * TN],
                True, True, waits=(("ve", b1), ("ve", ec)))
        m4 = MM(pb7[:, :], onesrow_s[:, :], rows_bf[0:1, 3 * TN:4 * TN],
                True, True, waits=(("ve", b2), ("ve", ec)))
        V(lambda v: v.tensor_copy(ab_s[:, 0:TN], pb6[:, :]), waits=(("pe", m3),))
        V(lambda v: v.tensor_copy(ab_s[:, TN:2 * TN], pb7[:, :]),
          waits=(("pe", m4),))
        return P.cnt("ve")

    def ln_apply_to(src, dst, ready):
        blks = {}
        first = {"vector": True, "gpsimd": True}
        for cb in range(NB):
            if cb % 2 == 0:
                w = (("ve", ready),) if first["vector"] else ()
                first["vector"] = False
                V(lambda v, c=cb: v.tensor_tensor(
                    fmsl(dst, c, 0, T), fmsl(src, c, 0, T), rb_s[:, :],
                    ALU.mult), waits=w)
                blks[cb] = ("ve", V(lambda v, c=cb: v.tensor_tensor(
                    fmsl(dst, c, 0, T), fmsl(dst, c, 0, T), ab_s[:, :],
                    ALU.subtract)))
            else:
                w = (("ve", ready),) if first["gpsimd"] else ()
                first["gpsimd"] = False
                PO(lambda g, c=cb: g.tensor_tensor(
                    fmsl(dst, c, 0, T), fmsl(src, c, 0, T), rb_s[:, :],
                    ALU.mult), waits=w)
                blks[cb] = ("po", PO(lambda g, c=cb: g.tensor_tensor(
                    fmsl(dst, c, 0, T), fmsl(dst, c, 0, T), ab_s[:, :],
                    ALU.subtract)))
        return blks

    ln1blk = {}
    if _PHASE_LIMIT >= 1:
        r = ln_prepare(xbf, sq_ac=True, blk_ready=d_xg)
        ln1blk = ln_apply_to(xbf, lnb, r)
        tap("ln1", lnb[:, :])
    ph1_ve = P.cnt("ve")

    def proj_fm(w_handle, src, dst, bias_col, kwaits=None, pre=None):
        blk_ac = {}

        def consume(j, idx, dtar):
            for fc in range(2):
                blk = j * 2 + fc
                for tt in range(TT):
                    b = next_bank()
                    ev = rot["evac"][b]
                    for k in range(16):
                        w = []
                        if j == 0 and fc == 0 and tt == 0 and kwaits is not None:
                            w.append(kwaits[k])
                        if k == 0:
                            w.append(dtar)
                            if ev is not None:
                                w.append(ev)
                        MM(p32[b][:, :],
                           wbuf(idx)[:, k * 256 + fc * 128:
                                       k * 256 + fc * 128 + 128],
                           fmsl(src, k, tt * TN, (tt + 1) * TN),
                           k == 0, k == 15, waits=tuple(w))
                    pe_c = P.cnt("pe")
                    cnt = A_(lambda s, b_=b, bl=blk, t_=tt: s.activation(
                        fmsl(dst, bl, t_ * TN, (t_ + 1) * TN), p32[b_][:, :],
                        AF.Identity, bias=bias_col[:, bl:bl + 1], scale=1.0),
                        waits=(("pe", pe_c),))
                    rot["evac"][b] = ("ac", cnt)
                    blk_ac[blk] = cnt
        tiles = [(w_handle[j, :, :, :].rearrange("k p f -> p k f"), 16 * 256)
                 for j in range(8)]
        stream(tiles, consume, pre=pre)
        return blk_ac

    if _PHASE_LIMIT >= 2:
        d_bv = DMA_W(bv_s[:, :], bv_h[:, :], sem="dBV",
                     waits=(("ve", ph1_ve), ("pe", P.cnt("pe"))))

        rot_reset()
        proj_fm(wk_h, lnb, kfm, bk_s, kwaits=ln1blk, pre=wk_pre)
        tap("kfm", kfm[:, :])

        def v_consume(j, idx, dtar):
            for tc in range(NCH):
                b = next_bank()
                ev = rot["evac"][b]
                for k in range(16):
                    w = []
                    if k == 0:
                        w.append(dtar)
                        if ev is not None:
                            w.append(ev)
                    MM(p32[b][:, 0:256],
                       fmsl(lnb, k, tc * 128, (tc + 1) * 128),
                       wbuf(idx)[:, k * 256:(k + 1) * 256],
                       k == 0, False, waits=tuple(w))
                MM(p32[b][:, 0:256], onesrow_s[:, :],
                   bv_s[0:1, j * 256:(j + 1) * 256], False, True,
                   waits=(("dBV", d_bv),))
                pe_c = P.cnt("pe")
                cnt = V(lambda v, b_=b, j_=j, t_=tc: v.tensor_copy(
                    vtm[:, t_ * D + j_ * 256: t_ * D + (j_ + 1) * 256],
                    p32[b_][:, 0:256]), waits=(("pe", pe_c),))
                rot["evac"][b] = ("ve", cnt)
        rot_reset()
        tiles = [(wv_h[j, :, :, :].rearrange("k p f -> p k f"), 16 * 256)
                 for j in range(8)]
        stream(tiles, v_consume)
        tap("vtm", vtm[:, :])
        ph2_pe = P.cnt("pe")
        barrier()

    att = {"inter": 0, "sdone": 0, "tr_evac": 0, "sc_evac": 0,
           "ao_evac": 0, "tr_done": 0, "u_prev": 0, "ladd": 0,
           "aoA": 0, "aoB": 0, "tab": 0, "zcast": 0}

    def chunk_L(c):
        if c == 0:
            att["tab"] = DMA_W(dkc8b[:, :], dkc8t_c[0, :, :], sem="dT8")
        for h in range(H):
            tr = trA if h < 8 else trB
            w = []
            if h == 0 and att["tr_evac"]:
                w.append(("ve", att["tr_evac"]))
            att["tr_done"] = TRP(tr[:, (h % 8) * 128:(h % 8 + 1) * 128],
                                 fmsl(kfm, h, c * 128, (c + 1) * 128),
                                 waits=tuple(w))
        w = [("pe", att["tr_done"]), ("dT8", att["tab"])]
        if att["u_prev"]:
            w.append(("pe", att["u_prev"]))
        V(lambda v: v.tensor_tensor(ktm_s[:, 0:1024], trA[:, :],
                                    dkc8b[:, 0:1024], ALU.mult),
          waits=tuple(w))
        att["tr_evac"] = V(lambda v: v.tensor_tensor(
            ktm_s[:, 1024:2048], trB[:, :], dkc8b[:, 1024:2048], ALU.mult))
        if c + 1 < NCH:
            att["tab"] = DMA_W(dkc8b[:, :], dkc8t_c[c + 1, :, :],
                               sem="dT8", waits=(("ve", att["tr_evac"]),))
        u_done = 0
        for h in range(H):
            w = []
            if h == 0:
                w.append(("ve", att["tr_evac"]))
                if att["ladd"]:
                    w.append(("ve", att["ladd"]))
            u_done = MM(p32[h // 4][:, (h % 4) * 128:(h % 4 + 1) * 128],
                        ktm_s[:, h * 128:(h + 1) * 128],
                        vtm[:, c * D + h * 128: c * D + (h + 1) * 128],
                        True, True, waits=tuple(w))
        att["u_prev"] = u_done
        for i in range(4):
            att["ladd"] = V(lambda v, i_=i: v.tensor_tensor(
                S_f[:, i_ * 512:(i_ + 1) * 512],
                S_f[:, i_ * 512:(i_ + 1) * 512], p32[i_][:, :], ALU.add),
                waits=(("pe", u_done),) if i == 0 else ())

    def emit_tr(c):
        for h in range(H):
            tr = trA if h < 8 else trB
            w = []
            if h == 0 and att["tr_evac"]:
                w.append((att.get("tr_evac_sem", "ve"), att["tr_evac"]))
            att["tr_done"] = TRP(tr[:, (h % 8) * 128:(h % 8 + 1) * 128],
                                 fmsl(kfm, h, c * 128, (c + 1) * 128),
                                 waits=tuple(w))

    def chunk_local(c, first_dw=None, first_dx=None):
        if c > 0:
            w = [("ve", att["sdone"]), ("pe", att["inter"])]
            if c == 1 and first_dx is not None:
                w.append(("dCC", first_dx))
            att["zcast"] = A_(lambda s: s.activation(Z_bf[:, :], S_f[:, :],
                                                     AF.Identity),
                              waits=tuple(w))
        if c == 0:
            emit_tr(c)
        w = [("pe", att["tr_done"])]
        if att["inter"]:
            w.append(("pe", att["inter"]))
        A_(lambda s: s.activation(ktm_s[:, 0:1024], trA[:, :], AF.Identity),
           waits=tuple(w))
        att["tr_evac"] = A_(lambda s: s.activation(
            ktm_s[:, 1024:2048], trB[:, :], AF.Identity))
        att["tr_evac_sem"] = "ac"
        kA = kB = 0
        for h in range(H):
            cnt = A_(lambda s, h_=h: s.activation(
                ktm_s[:, h_ * 128:(h_ + 1) * 128],
                ktm_s[:, h_ * 128:(h_ + 1) * 128], AF.Identity,
                scale=dkcol_s[:, h_:h_ + 1]))
            if h == 7:
                kA = cnt
            if h == 15:
                kB = cnt
        u_done = 0
        for h in range(H):
            w = []
            if h == 0:
                w.append(("ac", kA))
                if att["aoB"]:
                    w.append(("ve", att["aoB"]))
            if h == 8:
                w.append(("ac", kB))
            u_done = MM(p32[h // 4][:, (h % 4) * 128:(h % 4 + 1) * 128],
                        ktm_s[:, h * 128:(h + 1) * 128],
                        vtm[:, c * D + h * 128: c * D + (h + 1) * 128],
                        True, True, waits=tuple(w))
        scA = 0
        for h in range(8):
            pbank = pb6 if h < 4 else pb7
            w = []
            if h == 0:
                if att["aoA"]:
                    w.append(("ve", att["aoA"]))
                if first_dw is not None and c == 0:
                    w.append(("dAT", first_dw))
            scA = MM(pbank[:, (h % 4) * 128:(h % 4 + 1) * 128],
                     fmsl(kfm, h, c * 128, (c + 1) * 128),
                     fmsl(qfm, h, c * 128, (c + 1) * 128),
                     True, True, waits=tuple(w))
        if c + 1 < NCH:
            emit_tr(c + 1)
        w = [("pe", u_done)]
        if c > 0:
            w.append(("ac", att["zcast"]))
        smult = 0
        for h in range(H):
            smult = A_(lambda s, h_=h: s.activation(
                S_f[:, h_ * 128:(h_ + 1) * 128],
                S_f[:, h_ * 128:(h_ + 1) * 128], AF.Identity,
                scale=float(LAM[h_] ** 128.0)),
                waits=tuple(w) if h == 0 else ())
        for i in range(4):
            att["sdone"] = V(lambda v, i_=i: v.tensor_tensor(
                S_f[:, i_ * 512:(i_ + 1) * 512],
                S_f[:, i_ * 512:(i_ + 1) * 512], p32[i_][:, :], ALU.add),
                waits=(("ac", smult),) if i == 0 else ())
        scpA = 0
        for i in range(2):
            pbank = pb6 if i == 0 else pb7
            scpA = V(lambda v, p_=pbank, i_=i: v.tensor_tensor(
                ktm_s[:, i_ * 512:(i_ + 1) * 512], p_[:, :],
                dm_t[:, i_ * 512:(i_ + 1) * 512], ALU.mult),
                waits=(("pe", scA),) if i == 0 else ())
        scB = 0
        for h in range(8, 16):
            b = (h - 8) // 4
            w = [("ve", att["sdone"])] if h == 8 else []
            scB = MM(p32[b][:, (h % 4) * 128:(h % 4 + 1) * 128],
                     fmsl(kfm, h, c * 128, (c + 1) * 128),
                     fmsl(qfm, h, c * 128, (c + 1) * 128),
                     True, True, waits=tuple(w))
        qt = V(lambda v, c_=c: v.tensor_tensor(
            blk3(qfm, c_), blk3(qfm, c_), hview(dq_t), ALU.mult),
            waits=(("pe", scB),))
        scpB = 0
        for i in range(2):
            scpB = V(lambda v, i_=i: v.tensor_tensor(
                ktm_s[:, 1024 + i_ * 512:1024 + (i_ + 1) * 512],
                p32[i_][:, :], dm_t[:, 1024 + i_ * 512:1024 + (i_ + 1) * 512],
                ALU.mult), waits=(("pe", scB),) if i == 0 else ())
        lastA = 0
        for h in range(8):
            pbank = pb6 if h < 4 else pb7
            w = []
            if h == 0:
                w.append(("ve", scpA))
                w.append(("ve", qt))
                if c > 0:
                    w.append(("ac", att["zcast"]))
            if c > 0:
                MM(pbank[:, (h % 4) * 128:(h % 4 + 1) * 128],
                   Z_bf[:, h * 128:(h + 1) * 128],
                   fmsl(qfm, h, c * 128, (c + 1) * 128),
                   True, False, waits=tuple(w))
                lastA = MM(pbank[:, (h % 4) * 128:(h % 4 + 1) * 128],
                           vtm[:, c * D + h * 128: c * D + (h + 1) * 128],
                           ktm_s[:, h * 128:(h + 1) * 128], False, True)
            else:
                lastA = MM(pbank[:, (h % 4) * 128:(h % 4 + 1) * 128],
                           vtm[:, c * D + h * 128: c * D + (h + 1) * 128],
                           ktm_s[:, h * 128:(h + 1) * 128], True, True,
                           waits=tuple(w))
        for i in range(2):
            pbank = pb6 if i == 0 else pb7
            att["aoA"] = V(lambda v, p_=pbank, i_=i, c_=c: v.tensor_copy(
                lnb.rearrange("p (b t) -> p b t", b=NB)
                [:, 4 * i_:4 * i_ + 4, c_ * 128:(c_ + 1) * 128],
                p_[:, :].rearrange("p (h t) -> p h t", h=4)),
                waits=(("pe", lastA),) if i == 0 else ())
        lastB = 0
        for h in range(8, 16):
            b = 2 + (h - 8) // 4
            w = []
            if h == 8:
                w.append(("ve", scpB))
            if c > 0:
                MM(p32[b][:, (h % 4) * 128:(h % 4 + 1) * 128],
                   Z_bf[:, h * 128:(h + 1) * 128],
                   fmsl(qfm, h, c * 128, (c + 1) * 128),
                   True, False, waits=tuple(w))
                lastB = MM(p32[b][:, (h % 4) * 128:(h % 4 + 1) * 128],
                           vtm[:, c * D + h * 128: c * D + (h + 1) * 128],
                           ktm_s[:, h * 128:(h + 1) * 128], False, True)
            else:
                lastB = MM(p32[b][:, (h % 4) * 128:(h % 4 + 1) * 128],
                           vtm[:, c * D + h * 128: c * D + (h + 1) * 128],
                           ktm_s[:, h * 128:(h + 1) * 128], True, True,
                           waits=tuple(w))
        att["inter"] = lastB
        for i in range(2):
            att["aoB"] = V(lambda v, i_=i, c_=c: v.tensor_copy(
                lnb.rearrange("p (b t) -> p b t", b=NB)
                [:, 8 + 4 * i_:12 + 4 * i_, c_ * 128:(c_ + 1) * 128],
                p32[2 + i_][:, :].rearrange("p (h t) -> p h t", h=4)),
                waits=(("pe", lastB),) if i == 0 else ())
        att["ao_evac"] = att["aoB"]

    if _PHASE_LIMIT >= 3:
        V(lambda v: v.memset(S_f[:, :], 0.0))
        att["tr_done"] = att["tr_evac"] = att["u_prev"] = att["ladd"] = 0
        for c in range(NCH):
            chunk_L(c)
        ev = V(lambda v: v.tensor_copy(Z_bf[:, :], S_f[:, :]),
               waits=(("ve", att["ladd"]),))
        tap("Lloc", Z_bf[:, :])
        d_ccin = DMA_X(cc_in[:, :], Z_bf[:, :], sem="dCC",
                       waits=(("ve", ev),))
        ph3_pe = P.cnt("pe")
        ph3_ve = P.cnt("ve")
        barrier(extra_excl=("dX",))

    if _PHASE_LIMIT >= 4:
        ws["bufs"] = (wbA, wbB)
        ws["nb"] = 2
        ws["i"] = 0
        ws["consumed"] = {}
        ws["hold"] = (("pe", ph2_pe),)
        rot_reset()
        proj_fm(wq_h, lnb, qfm, bq_s)
        ws["hold"] = ()
        ph4_pe = P.cnt("pe")
        tap("qfm", qfm[:, :])
        DMA_W(dq_t[:, :], dq_c[:, :], sem="dAT",
              waits=(("pe", ph3_pe), ("ve", ph3_ve)))
        d_attc = DMA_W(dm_t[:, :], dm_c[:, :], sem="dAT")
    if _PHASE_LIMIT >= 3:
        P.emit("gpsimd",
               lambda g: g.collective_compute(
                   "AllGather", ALU.bypass,
                   replica_groups=[[0, 1, 2, 3, 4, 5, 6, 7]],
                   ins=[cc_in[:, :]], outs=[cc_out[:, :]]),
               waits=(("dCC", d_ccin),), inc="cc", inc_n=1)
        barrier()

    if _PHASE_LIMIT >= 6:
        V(lambda v: v.memset(S_f[:, :], 0.0))
        att["tr_done"] = att["tr_evac"] = att["sc_evac"] = 0
        att["inter"] = att["ao_evac"] = att["aoA"] = att["aoB"] = 0
        for c in range(NCH):
            chunk_local(c, first_dw=d_attc if c == 0 else None,
                        first_dx=d_ccin)

    comb_end = 0
    if _PHASE_LIMIT >= 5:
        Lbs = [wbA[:, 0:2048], wbB[:, 0:2048]]
        Cbs = [wbA[:, 2048:4096], wbB[:, 2048:4096]]
        z0 = PO(lambda g: g.memset(sinit_bf[:, :], 0.0))
        prev = z0
        muls = {}
        adds = {}
        for j in range(8):
            p = j % 2
            sem = "dL5a" if p == 0 else "dL5b"
            w = [("cc", 1)]
            if j == 0:
                w.append(("pe", ph4_pe))
            if j >= 2:
                w.append(("po", adds[j - 2]))
            dL = DMA_X(Lbs[p], cc_out[j * 128:(j + 1) * 128, :],
                       waits=tuple(w), sem=sem)
            dC = DMA_X(Cbs[p], coeff_h[j, :, :], sem=sem)
            muls[j] = PO(lambda g, p_=p: g.tensor_tensor(Lbs[p_], Lbs[p_],
                                                         Cbs[p_], ALU.mult),
                         waits=((sem, dC),))
            prev = PO(lambda g, p_=p: g.tensor_tensor(
                sinit_bf[:, :], sinit_bf[:, :], Lbs[p_], ALU.add),
                waits=(("po", muls[j]), ("po", prev)))
            adds[j] = prev
        comb_end = prev
        tap("Sinit", sinit_bf[:, :])

    if _PHASE_LIMIT >= 6:
        prev_mm = att["inter"]
        prev_ev = att["ao_evac"]
        for c in range(NCH):
            zc = 0
            for h in range(H):
                w = []
                if h == 0:
                    w.append(("po", comb_end))
                    w.append(("pe", prev_mm))
                zc = A_(lambda s, h_=h, c_=c: s.activation(
                    Z_bf[:, h_ * 128:(h_ + 1) * 128],
                    sinit_bf[:, h_ * 128:(h_ + 1) * 128], AF.Identity,
                    scale=float(LAM[h_] ** (128.0 * c_))),
                    waits=tuple(w))
            mmc = 0
            for h in range(H):
                w = []
                if h == 0:
                    w.append(("ac", zc))
                    w.append(("ve", prev_ev))
                mmc = MM(p32[h // 4][:, (h % 4) * 128:(h % 4 + 1) * 128],
                         Z_bf[:, h * 128:(h + 1) * 128],
                         fmsl(qfm, h, c * 128, (c + 1) * 128),
                         True, True, waits=tuple(w))
            prev_mm = mmc
            for i in range(4):
                prev_ev = V(lambda v, i_=i, c_=c: v.tensor_tensor(
                    lnb.rearrange("p (b t) -> p b t", b=NB)
                    [:, 4 * i_:4 * i_ + 4, c_ * 128:(c_ + 1) * 128],
                    lnb.rearrange("p (b t) -> p b t", b=NB)
                    [:, 4 * i_:4 * i_ + 4, c_ * 128:(c_ + 1) * 128],
                    p32[i_][:, :].rearrange("p (h t) -> p h t", h=4),
                    ALU.add), waits=(("pe", mmc),) if i == 0 else ())
        tap("attn", lnb[:, :])
        barrier()

    if _PHASE_LIMIT >= 7:
        rot_reset()
        ws["hold"] = (("po", comb_end),)
        blk7 = proj_fm(wo_h, lnb, qfm, bo_s)
        ws["hold"] = ()
        tap("yattn", qfm[:, :])

    if _PHASE_LIMIT >= 8:
        r = ln_prepare(qfm, sq_ac=True,
                       blk_ready={cb: ("ac", blk7[cb]) for cb in blk7})
        respo = {}
        for cb in range(NB):
            tb = outst[:, (cb % 2) * T:(cb % 2 + 1) * T]
            w = []
            if cb == 0:
                w.append(("ve", r))
            if cb >= 2:
                w.append(("po", respo[cb - 2]))
            V(lambda v, c=cb, o=tb: v.tensor_tensor(
                o, fmsl(qfm, c, 0, T), rb_s[:, :], ALU.mult), waits=tuple(w))
            dv = V(lambda v, o=tb: v.tensor_tensor(o, o, ab_s[:, :],
                                                   ALU.subtract))
            respo[cb] = PO(lambda g, c=cb, o=tb: g.tensor_tensor(
                fmsl(xbf, c, 0, T), fmsl(xbf, c, 0, T), o, ALU.add),
                waits=(("ve", dv),))
        tap("x2", xbf[:, :])

    if _PHASE_LIMIT >= 9:
        r = ln_prepare(xbf, sq_ac=True,
                       blk_ready={cb: ("po", respo[cb]) for cb in respo})
        ln2blk = ln_apply_to(xbf, lnb, r)
        tap("ln2", lnb[:, :])

    if _PHASE_LIMIT >= 10:
        sg_prev = None
        ev_half = 0
        we_done = 0
        for tc in range(NCH):
            last = 0
            for k in range(16):
                w = []
                if tc == 0:
                    w.append(ln2blk[k])
                if k == 0:
                    if tc == 0:
                        w.append(("dC1", d_gatew))
                    if sg_prev is not None:
                        w.append(("ac", sg_prev))
                last = MM(pb6[:, 0:8], fmsl(lnb, k, tc * 128, (tc + 1) * 128),
                          gatew_s[:, k * 8:(k + 1) * 8], k == 0, k == 15,
                          waits=tuple(w))
            sg = A_(lambda s, t_=tc: s.activation(
                gsig_s[:, t_ * 8:(t_ + 1) * 8], pb6[:, 0:8], AF.Sigmoid),
                waits=(("pe", last),))
            sg_prev = sg
            sig = gsig_s[:, tc * 8:(tc + 1) * 8]
            e = V(lambda v, s_=sig: v.tensor_tensor(g1_s[:, :], s_,
                                                    gbias_s[:, :], ALU.add),
                  waits=(("ac", sg),))
            e = V(lambda v: v.tensor_reduce(g3_s[:, :], g1_s[:, :], AX.X,
                                            ALU.max), waits=(("ve", e),))
            e = V(lambda v: v.tensor_scalar(g2_s[:, :], g1_s[:, :],
                                            g3_s[:, :], None, ALU.is_ge),
                  waits=(("ve", e),))
            e = V(lambda v: v.scalar_tensor_tensor(
                g1_s[:, :], g2_s[:, :], BIGNEG, g1_s[:, :], ALU.mult, ALU.add),
                waits=(("ve", e),))
            e = V(lambda v: v.tensor_reduce(g4_s[:, :], g1_s[:, :], AX.X,
                                            ALU.max), waits=(("ve", e),))
            e = V(lambda v: v.tensor_scalar(g1_s[:, :], g1_s[:, :],
                                            g4_s[:, :], None, ALU.is_ge),
                  waits=(("ve", e),))
            e = V(lambda v: v.tensor_tensor(g2_s[:, :], g2_s[:, :],
                                            g1_s[:, :], ALU.add),
                  waits=(("ve", e),))
            e = V(lambda v, s_=sig: v.tensor_tensor(g1_s[:, :], s_,
                                                    g2_s[:, :], ALU.mult),
                  waits=(("ve", e),))
            e = V(lambda v: v.tensor_reduce(g3_s[:, :], g1_s[:, :], AX.X,
                                            ALU.add), waits=(("ve", e),))
            e = V(lambda v: v.reciprocal(g4_s[:, :], g3_s[:, :]),
                  waits=(("ve", e),))
            e = V(lambda v, t_=tc: v.tensor_scalar(
                wn_s[:, t_ * 8:(t_ + 1) * 8], g1_s[:, :], g4_s[:, :], None,
                ALU.mult), waits=(("ve", e),))
            w = [("ve", e)]
            if tc == 4:
                w.append(("ve", ev_half))
            tr_cnt = MM(pb7[0:8, (tc % 4) * 128:(tc % 4 + 1) * 128],
                        wn_s[:, tc * 8:(tc + 1) * 8], ident_s[:, :],
                        True, True, waits=tuple(w))
            if tc == 3:
                ev_half = V(lambda v: v.tensor_copy(we_fm[:, 0:512],
                                                    pb7[0:8, :]),
                            waits=(("pe", tr_cnt),))
            if tc == 7:
                we_done = V(lambda v: v.tensor_copy(we_fm[:, 512:1024],
                                                    pb7[0:8, :]),
                            waits=(("pe", tr_cnt),))
        tap("wefm", we_fm[:, :])

    def up_pairs(tiles, nblk, we_par, kwaits=None, we_wait=None,
                 mid_hook=None, late_hook=None):
        hv_prev = {0: None, 1: None}
        po_last = 0
        po_blk = {}
        hv_blk = {}
        first_po = [we_wait]

        def consume(j, idx, dtar):
            nonlocal po_last
            if mid_hook is not None and j == nblk // 2:
                mid_hook()
            if late_hook is not None and j >= nblk - 2:
                late_hook()
            for tt in range(TT):
                bg = next_bank()
                bu = next_bank()
                evg = rot["evac"][bg]
                evu = rot["evac"][bu]
                for k in range(16):
                    w = []
                    if j == 0 and tt == 0 and kwaits is not None:
                        w.append(kwaits[k])
                    if k == 0:
                        w.append(dtar)
                        if evg is not None:
                            w.append(evg)
                    MM(p32[bg][:, :], wbuf(idx)[:, k * 256:k * 256 + 128],
                       fmsl(lnb, k, tt * TN, (tt + 1) * TN),
                       k == 0, k == 15, waits=tuple(w))
                for k in range(16):
                    w = []
                    if k == 0 and evu is not None:
                        w.append(evu)
                    MM(p32[bu][:, :],
                       wbuf(idx)[:, k * 256 + 128:k * 256 + 256],
                       fmsl(lnb, k, tt * TN, (tt + 1) * TN),
                       k == 0, k == 15, waits=tuple(w))
                pe_c = P.cnt("pe")
                tmp = tmpA if tt == 0 else tmpB
                w = [("pe", pe_c)]
                if hv_prev[tt] is not None:
                    w.append(("ve", hv_prev[tt]))
                sgc = A_(lambda s, b_=bg, t_=tmp: s.activation(
                    t_[:, 0:TN], p32[b_][:, :], AF.Silu), waits=tuple(w))
                rot["evac"][bg] = ("ac", sgc)
                hv = V(lambda v, b_=bu, j_=j, t_=tt, tm=tmp: v.tensor_tensor(
                    hbuf[:, j_ * T + t_ * TN: j_ * T + (t_ + 1) * TN],
                    tm[:, 0:TN], p32[b_][:, :], ALU.mult),
                    waits=(("ac", sgc),))
                rot["evac"][bu] = ("ve", hv)
                hv_prev[tt] = hv
                hv_blk[j] = hv
                if we_par is not None:
                    wpo = [("ve", hv)]
                    if first_po[0] is not None:
                        wpo.append(first_po[0])
                        first_po[0] = None
                    po_last = PO(lambda g, j_=j, t_=tt: g.tensor_tensor(
                        hbuf[:, j_ * T + t_ * TN: j_ * T + (t_ + 1) * TN],
                        hbuf[:, j_ * T + t_ * TN: j_ * T + (t_ + 1) * TN],
                        we_b[we_par][:, t_ * TN:(t_ + 1) * TN], ALU.mult),
                        waits=tuple(wpo))
                    po_blk[j] = po_last
        stream(tiles, consume)
        return P.cnt("ve"), po_last, (po_blk if we_par is not None else hv_blk)

    def down_proj(tiles, nk, fpt, first_write, h_sem, h_blk, pre=None):
        blk_ve = {}

        def consume(j, idx, dtar):
            for fc in range(fpt):
                blk = j * fpt + fc
                for tt in range(TT):
                    b = next_bank()
                    ev = rot["evac"][b]
                    for k in range(nk):
                        w = []
                        if k == 0:
                            w.append(dtar)
                            if ev is not None:
                                w.append(ev)
                        if j == 0 and fc == 0 and tt == 0 and k in h_blk:
                            w.append((h_sem, h_blk[k]))
                        MM(p32[b][:, :],
                           wbuf(idx)[:, k * (128 * fpt) + fc * 128:
                                       k * (128 * fpt) + fc * 128 + 128],
                           hbuf[:, k * T + tt * TN: k * T + (tt + 1) * TN],
                           k == 0, k == nk - 1, waits=tuple(w))
                    pe_c = P.cnt("pe")
                    if first_write:
                        cnt = V(lambda v, b_=b, bl=blk, t_=tt: v.tensor_copy(
                            fmsl(acc, bl, t_ * TN, (t_ + 1) * TN),
                            p32[b_][:, :]), waits=(("pe", pe_c),))
                    else:
                        cnt = V(lambda v, b_=b, bl=blk, t_=tt: v.tensor_tensor(
                            fmsl(acc, bl, t_ * TN, (t_ + 1) * TN),
                            fmsl(acc, bl, t_ * TN, (t_ + 1) * TN),
                            p32[b_][:, :], ALU.add), waits=(("pe", pe_c),))
                    rot["evac"][b] = ("ve", cnt)
                    blk_ve[blk] = cnt
        stream(tiles, consume, pre=pre)
        return blk_ve

    if _PHASE_LIMIT >= 11:
        ws["bufs"] = (wbA, wbB, wb3)
        ws["nb"] = 3
        ws["i"] = 0
        ws["consumed"] = {}
        ws["hold"] = (("pe", P.cnt("pe")), ("ve", P.cnt("ve")))
        rot_reset()
        gw_tiles = [(gw_h[j, :, :, :].rearrange("k p f -> p k f"), 16 * 256)
                    for j in range(22)]
        dw_tiles = [(dwt_h[j, :, :, :].rearrange("k p f -> p k f"), 22 * 128)
                    for j in range(16)]
        dw_pre = []

        def lh11():
            dw_pre.append(wload(*dw_tiles[len(dw_pre)]))
        hr, _, hblk = up_pairs(gw_tiles, 22, None, kwaits=ln2blk,
                               late_hook=lh11)
        ws["hold"] = ()
        rot_reset()
        down_proj(dw_tiles, 22, 1, True, "ve", hblk, pre=dw_pre)
        tap("shared", acc[:, :])

    if _PHASE_LIMIT >= 12:
        ws["bufs"] = (wbA, wbB, wb3, wb4)
        ws["nb"] = 4
        ws["i"] = 0
        ws["consumed"] = {}
        ws["hold"] = (("pe", P.cnt("pe")),)
        mb_prev = 0
        wready = {}
        po_hist = {}

        def we_prep(e):
            nonlocal mb_prev
            par = e % 2
            w = [("ve", we_done)]
            if mb_prev:
                w.append(("pe", mb_prev))
            if e >= 2 and (e - 2) in po_hist:
                w.append(("po", po_hist[e - 2]))
            dwe = DMA_X(we_row[:, :], we_fm[e:e + 1, :], waits=tuple(w),
                        sem="dWE")
            wmb = [("dWE", dwe)]
            if (e - 1) in wready:
                wmb.append(("ve", wready[e - 1]))
            mb1 = MM(pb6[:, :], onesrow_s[:, :], we_row[0:1, 0:TN],
                     True, True, waits=tuple(wmb))
            mb2 = MM(pb7[:, :], onesrow_s[:, :], we_row[0:1, TN:2 * TN],
                     True, True)
            mb_prev = mb2
            V(lambda v, p_=par: v.tensor_copy(we_b[p_][:, 0:TN], pb6[:, :]),
              waits=(("pe", mb1),))
            wready[e] = V(lambda v, p_=par: v.tensor_copy(
                we_b[p_][:, TN:2 * TN], pb7[:, :]), waits=(("pe", mb2),))

        we_prep(0)
        dn_blk = {}
        for e in range(NE):
            par = e % 2
            rot_reset()
            up_tiles = [(up_h[e, j, :, :, :].rearrange("k p f -> p k f"),
                         16 * 256) for j in range(11)]
            dn_tiles = [(dn_h[e, j, :, :, :].rearrange("k p f -> p k f"),
                         11 * 256) for j in range(8)]
            hook = (lambda e_=e: we_prep(e_ + 1)) if e + 1 < NE else None
            dn_pre = []

            def lhe(dt_=dn_tiles, dp_=dn_pre):
                dp_.append(wload(*dt_[len(dp_)]))
            hr, hpo, hblk = up_pairs(up_tiles, 11, par,
                                     we_wait=("ve", wready[e]),
                                     mid_hook=hook, late_hook=lhe)
            if e == 0:
                ws["hold"] = ()
            po_hist[e] = hpo
            rot_reset()
            dn_blk = down_proj(dn_tiles, 11, 2, False, "po", hblk,
                               pre=dn_pre)
        tap("ffn", acc[:, :])

    if _PHASE_LIMIT >= 13:
        r = ln_prepare(acc, sq_ac=True,
                       blk_ready={cb: ("ve", dn_blk[cb]) for cb in dn_blk})
        dds = {}
        first_po13 = True
        for cb in range(NB):
            ost = outst[:, (cb % 2) * T:(cb % 2 + 1) * T]
            w = []
            if cb == 0:
                w.append(("ve", r))
            osem = "dOa" if cb % 2 == 0 else "dOb"
            if cb >= 2:
                w.append((osem, dds[cb - 2]))
            if cb % 2 == 0:
                V(lambda v, c=cb, o=ost: v.tensor_tensor(
                    o, fmsl(acc, c, 0, T), rb_s[:, :], ALU.mult),
                  waits=tuple(w))
                V(lambda v, o=ost: v.tensor_tensor(o, o, ab_s[:, :],
                                                   ALU.subtract))
                dv = V(lambda v, c=cb, o=ost: v.tensor_tensor(
                    o, o, fmsl(xbf, c, 0, T), ALU.add))
                dsem = "ve"
            else:
                if first_po13:
                    w.append(("ve", r))
                    first_po13 = False
                w.append(("ve", dn_blk[cb]))
                PO(lambda g, c=cb, o=ost: g.tensor_tensor(
                    o, fmsl(acc, c, 0, T), rb_s[:, :], ALU.mult),
                   waits=tuple(w))
                PO(lambda g, o=ost: g.tensor_tensor(o, o, ab_s[:, :],
                                                    ALU.subtract))
                dv = PO(lambda g, c=cb, o=ost: g.tensor_tensor(
                    o, o, fmsl(xbf, c, 0, T), ALU.add))
                dsem = "po"
            dds[cb] = DMA_X(y_h[cb * 128:(cb + 1) * 128, :], ost,
                            waits=((dsem, dv),), sem=osem)

    WAIT("sync", tuple((s, P.cnt(s)) for s in DMA_SEMS if P.cnt(s) > 0))
    WAIT("gpsimd", tuple((s, P.cnt(s))
                         for s in ("dWa", "dWb", "dWc", "dWd") if P.cnt(s) > 0))

    from contextlib import ExitStack
    with ExitStack() as ctx:
        sems = {s: ctx.enter_context(nc.semaphore(s)) for s in SEMS}
        block = ctx.enter_context(nc.Block())
        for eng_name in ENGS:
            def mkbody(en):
                def body(eng):
                    for (builder, waits, inc, inc_n) in P.ops[en]:
                        for (s, v) in waits:
                            eng.wait_ge(sems[s], v)
                        if builder is not None:
                            inst = builder(eng)
                            if inc is not None:
                                inst.then_inc(sems[inc], inc_n)
                return body
            getattr(block, eng_name)(mkbody(eng_name))
    return nc


def _prep_weights(w_qkv, b_qkv, w_out, b_out, gate_w, gate_bias, shared_gw,
                  shared_dw, up_w, down_w):
    C = np.ascontiguousarray

    def fm_tiles(WT):
        F = WT.shape[1]
        return C(WT.reshape(16, 128, F // 256, 256).transpose(2, 0, 1, 3))

    d = {}
    d["wq"] = fm_tiles(w_qkv[0:2048].T)
    d["wk"] = fm_tiles(w_qkv[2048:4096].T)
    d["wv"] = fm_tiles(w_qkv[4096:6144].T)
    d["wo"] = fm_tiles(w_out.T)
    d["bq"] = C(b_qkv[0:2048].reshape(16, 128).T)
    d["bk"] = C(b_qkv[2048:4096].reshape(16, 128).T)
    d["bv"] = C(b_qkv[4096:6144].reshape(1, 2048))
    d["bo"] = C(b_out.reshape(16, 128).T)
    A = np.asarray(shared_gw).T
    y = A[:, 0:2816].reshape(16, 128, 22, 128)
    g = A[:, 2816:5632].reshape(16, 128, 22, 128)
    gwt = np.empty((22, 16, 128, 256), np.float32)
    gwt[..., 0:128] = g.transpose(2, 0, 1, 3)
    gwt[..., 128:256] = y.transpose(2, 0, 1, 3)
    d["gw"] = C(gwt)
    dwT = np.asarray(shared_dw).T
    d["dwt"] = C(dwT.reshape(22, 128, 16, 128).transpose(2, 0, 1, 3))
    up = np.empty((NE, 11, 16, 128, 256), np.float32)
    dn = np.empty((NE, 8, 11, 128, 256), np.float32)
    for e in range(NE):
        UT = np.asarray(up_w[e]).T
        ge = UT[:, 0:1408].reshape(16, 128, 11, 128)
        ue = UT[:, 1408:2816].reshape(16, 128, 11, 128)
        up[e, ..., 0:128] = ge.transpose(2, 0, 1, 3)
        up[e, ..., 128:256] = ue.transpose(2, 0, 1, 3)
        DT = np.asarray(down_w[e]).T
        dn[e] = DT.reshape(11, 128, 8, 256).transpose(2, 0, 1, 3)
    d["up"] = C(up)
    d["dn"] = C(dn)
    gT = np.asarray(gate_w).T.reshape(16, 128, 8)
    d["gatew"] = C(gT.transpose(1, 0, 2))
    d["gbias"] = C(np.broadcast_to(np.asarray(gate_bias)[None, :],
                                   (128, 8)).astype(np.float32))
    return d


def kernel(L0_attn_x, L0_attn_w_qkv, L0_attn_b_qkv, L0_attn_w_out,
           L0_attn_b_out, L0_ffn_gate_w, L0_ffn_gate_bias, L0_ffn_shared_gw,
           L0_ffn_shared_dw, L0_ffn_up_w, L0_ffn_down_w):
    C = np.ascontiguousarray
    args = [np.asarray(a) for a in
            (L0_attn_x, L0_attn_w_qkv, L0_attn_b_qkv, L0_attn_w_out,
             L0_attn_b_out, L0_ffn_gate_w, L0_ffn_gate_bias, L0_ffn_shared_gw,
             L0_ffn_shared_dw, L0_ffn_up_w, L0_ffn_down_w)]
    x = args[0]
    wd = _prep_weights(*args[1:])
    nc = build_program()
    in_maps = []
    for core in range(8):
        b, seg = core // 4, core % 4
        xT = C(x[b, seg * T:(seg + 1) * T, :].T.reshape(NB, 128, T))
        coeff = np.zeros((8, 128, 2048), np.float64)
        for j in range(8):
            bj, segj = j // 4, j % 4
            if bj == b and segj < seg:
                for h in range(H):
                    coeff[j, :, h * 128:(h + 1) * 128] = \
                        LAM[h] ** (1024.0 * (seg - segj - 1))
        m = dict(wd)
        m["xT"] = xT
        import ml_dtypes as _mld
        m["coeff"] = coeff.astype(_mld.bfloat16)
        in_maps.append(m)
    res = run_bass_kernel_spmd(nc, in_maps, core_ids=list(range(8)))
    globals()['_LAST_RES'] = res
    out = np.empty((B, N, D), np.float32)
    for core in range(8):
        b, seg = core // 4, core % 4
        out[b, seg * T:(seg + 1) * T, :] = res.results[core]["y"].T
    return out
